# version 02
# baseline (speedup 1.0000x reference)
import numpy as np
import ml_dtypes
from contextlib import ExitStack

import concourse.bass as bass
import concourse.mybir as mybir
from concourse.bass_utils import run_bass_kernel_spmd
from concourse.bass_core_rust import IndirectOffsetOnAxis

F32 = mybir.dt.float32
BF16 = mybir.dt.bfloat16
I32 = mybir.dt.int32
U32 = mybir.dt.uint32
AF = mybir.ActivationFunctionType
ALU = mybir.AluOpType
AX = mybir.AxisListType.X
BF = ml_dtypes.bfloat16

T, C, H4, E = 2048, 1024, 4096, 8
CAP, CAPP = 614, 640
NT, NCC, NHC = T // 128, C // 128, H4 // 128
NFFT, NW = 512, T // 256
HGRP = 512
NHG = H4 // HGRP
BIG = float(1 << 20)
EPS = 1e-5
SGRPS = [(0, 512), (512, 102)]


class Sem:
    __slots__ = ("h", "v")

    def __init__(self, ctx, nc, name):
        self.h = ctx.enter_context(nc.semaphore(name))
        self.v = 0

    def inc(self, inst, n=1):
        inst.then_inc(self.h, n)
        self.v += n
        return self.v


def build_nc():
    nc = bass.Bass("TRN2", target_bir_lowering=False)

    d = {}
    def din(name, shape, dt=F32):
        d[name] = nc.dram_tensor(name, shape, dt, kind="ExternalInput").ap()
        return d[name]

    dx = din("x", [T, C])
    din("rw_col", [128, NCC * E])
    din("rb_row", [1, E])
    din("iota_e", [128, E])
    din("tri_bf", [128, 128], BF16)
    din("ones_bf", [128, 128], BF16)
    din("wfc", [E, C, H4])
    din("wg", [E, C, H4])
    din("wp", [E, H4, C])
    din("bfc_col", [128, E * NHC])
    din("bg_col", [128, E * NHC])
    din("bp_col", [128, E * NCC])
    din("w_inT", [C, 2 * C])
    din("b_in_row", [1, 2 * C])
    din("w_outT", [2 * C, C])
    din("b_out_col", [128, NCC])
    din("wab_rev", [2, 256, C])
    din("b_a_col", [128, NCC])
    din("b_b_col", [128, NCC])
    din("rc", [NFFT, 256], BF16)
    din("rs", [NFFT, 256], BF16)
    din("ric", [256, 256], BF16)
    din("ris", [256, 256], BF16)
    dout = nc.dram_tensor("out", [T, C], F32, kind="ExternalOutput").ap()
    dxrows = nc.dram_tensor("xrows", [E * CAPP + 1, C], BF16, kind="Internal").ap()
    dorows = nc.dram_tensor("orows", [E * CAPP, C], BF16, kind="Internal").ap()

    ctx = ExitStack()
    S = lambda name, shape, dt=F32: ctx.enter_context(nc.sbuf_tensor(name, shape, dt))

    X = S("X", [128, NT * C])
    IDB = S("IDB", [128, 128], BF16)
    IDF = S("IDF", [128, 128], F32)
    ONESB = S("ONESB", [1, 128], BF16)
    ONESF = S("ONESF", [1, 128], F32)
    TRI = S("TRI", [128, 128], BF16)
    ONESC = S("ONESC", [128, 128], BF16)
    IOTAE = S("IOTAE", [128, E], F32)
    RWC = S("RWC", [128, NCC * E], F32)
    RBR = S("RBR", [1, E], F32)
    BNS = S("BNS", [128, 12], F32)
    MEANS = S("MEANS", [128, NT], F32)
    VARS = S("VARS", [128, NT], F32)
    NEGM = S("NEGM", [128, NT], F32)
    RSTD = S("RSTD", [128, NT], F32)
    LG = S("LG", [128, 2 * E], F32)
    MX8 = S("MX8", [128, 8], F32)
    IDXU = S("IDXU", [128, 8], U32)
    EF = S("EF", [128, 2], F32)
    ECOLS = S("ECOLS", [128, 2 * NT], F32)
    DSUB = S("DSUB", [128, NT], F32)
    P0T = S("P0T", [128, NT], F32)
    HP0 = S("HP0", [128, NT], F32)
    HP1 = S("HP1", [128, NT], F32)
    OH = S("OH", [128, NT * 16], BF16)
    OHF = S("OHF", [128, 16], F32)
    CST = S("CST", [128, 16], F32)
    TOTS = S("TOTS", [128, 16], F32)
    TMPA = S("TMPA", [128, 16], F32)
    TMPB = S("TMPB", [128, 16], F32)
    POSC = S("POSC", [128, 8], F32)
    SIDX = S("SIDX", [128, 2 * NT], I32)
    RIDX = S("RIDX", [128, 2 * NT], I32)
    BFCC = S("BFCC", [128, E * NHC], F32)
    BGC = S("BGC", [128, E * NHC], F32)
    BPC = S("BPC", [128, E * NCC], F32)
    DUM = S("DUM", [1, 16], F32)

    sX = Sem(ctx, nc, "sX")
    sW = Sem(ctx, nc, "sW")
    sSC = Sem(ctx, nc, "sSC")
    sER = Sem(ctx, nc, "sER")
    sOR = Sem(ctx, nc, "sOR")
    sGA = Sem(ctx, nc, "sGA")
    sPE = Sem(ctx, nc, "sPE")
    sDV = Sem(ctx, nc, "sDV")
    sAC = Sem(ctx, nc, "sAC")
    sPO = Sem(ctx, nc, "sPO")
    sST = Sem(ctx, nc, "sST")
    sBAR = Sem(ctx, nc, "sBAR")

    blk = nc.Block().__enter__()
    TEN, VEC, SCA, GPS, SYN = blk.tensor, blk.vector, blk.scalar, blk.gpsimd, blk.sync

    @SYN
    def _(sync):
        sync.dma_start(out=X[:].rearrange("p (i c) -> p i c", i=NT),
                       in_=dx.rearrange("(i p) c -> p i c", p=128)).then_inc(sX.h, 16)
        sX.v += 16
        for name, dst in [("rw_col", RWC), ("rb_row", RBR), ("iota_e", IOTAE),
                          ("tri_bf", TRI), ("ones_bf", ONESC),
                          ("bfc_col", BFCC), ("bg_col", BGC), ("bp_col", BPC)]:
            sync.dma_start(out=dst[:], in_=d[name][:]).then_inc(sX.h, 16)
            sX.v += 16
    setup_loads = sX.v

    @GPS
    def _(gpsimd):
        gpsimd.memset(IDB[:], 0.0)
        gpsimd.affine_select(out=IDB[:], in_=IDB[:], compare_op=ALU.not_equal,
                             fill=1.0, base=0, pattern=[[-1, 128]], channel_multiplier=1)
        gpsimd.memset(IDF[:], 0.0)
        gpsimd.affine_select(out=IDF[:], in_=IDF[:], compare_op=ALU.not_equal,
                             fill=1.0, base=0, pattern=[[-1, 128]], channel_multiplier=1)
        gpsimd.memset(ONESB[:], 1.0)
        inst = gpsimd.memset(ONESF[:], 1.0)
        sPO.inc(inst)
    setup_po = sPO.v

    def barrier():
        for dec, kind in ((TEN, "n"), (VEC, "v"), (SCA, "s"), (GPS, "v"), (SYN, "n")):
            @dec
            def _(eng, kind=kind):
                if kind == "v":
                    inst = eng.memset(DUM[0:1, 0:4], 0.0)
                elif kind == "s":
                    inst = eng.activation(out=DUM[0:1, 4:8], in_=DUM[0:1, 0:4], func=AF.Copy)
                else:
                    inst = eng.nop()
                inst.then_inc(sBAR.h, 1)
        sBAR.v += 5
        tgt = sBAR.v
        for dec in (TEN, VEC, SCA, GPS, SYN):
            @dec
            def _(eng):
                eng.wait_ge(sBAR.h, tgt)

    def emit_ln(first=False):
        @VEC
        def _(vector):
            if first:
                vector.wait_ge(sX.h, setup_loads)
                vector.wait_ge(sPO.h, setup_po)
            for i in range(NT):
                xt = X[:, i * C:(i + 1) * C]
                vector.bn_stats(out=BNS[:, 0:6], in_=xt[:, 0:512])
                inst = vector.bn_stats(out=BNS[:, 6:12], in_=xt[:, 512:1024])
                sDV.inc(inst)
                vector.wait_ge(sDV.h, sDV.v)
                inst = vector.bn_aggr(out=BNS[:, 0:2], in_=BNS[:, 0:12])
                sDV.inc(inst)
                vector.wait_ge(sDV.h, sDV.v)
                vector.tensor_copy(out=MEANS[:, i:i + 1], in_=BNS[:, 0:1])
                inst = vector.tensor_scalar(out=VARS[:, i:i + 1], in_=BNS[:, 1:2],
                                            scalar1=float(EPS), scalar2=None,
                                            op0=ALU.add) if False else \
                    vector.tensor_single_scalar(out=VARS[:, i:i + 1], in_=BNS[:, 1:2],
                                                scalar=float(EPS), op=ALU.add)
                sDV.inc(inst)
        stats_done = sDV.v

        @SCA
        def _(scalar):
            scalar.wait_ge(sDV.h, stats_done)
            inst = scalar.activation(out=VARS[:, 0:NT], in_=VARS[:, 0:NT],
                                     func=AF.Sqrt)
            sAC.inc(inst)
        sqrt_done = sAC.v

        @VEC
        def _(vector):
            vector.wait_ge(sAC.h, sqrt_done)
            vector.reciprocal(out=RSTD[:, 0:NT], in_=VARS[:, 0:NT])
            inst = vector.tensor_scalar(out=NEGM[:, 0:NT], in0=MEANS[:, 0:NT],
                                        scalar1=-1.0, scalar2=None, op0=ALU.mult)
            sDV.inc(inst)
            vector.wait_ge(sDV.h, sDV.v)
        return sDV.v

    def emit_ffn(call):
        tag = f"f{call}"
        fctx = ExitStack()
        fS = lambda n, shape, dt=F32: fctx.enter_context(nc.sbuf_tensor(f"{n}_{tag}", shape, dt))
        WFCB = [fS("WFCB0", [128, NCC * HGRP], BF16), fS("WFCB1", [128, NCC * HGRP], BF16)]
        WGB = [fS("WGB0", [128, NCC * HGRP], BF16), fS("WGB1", [128, NCC * HGRP], BF16)]
        WPB = [fS("WPB0", [128, NHC * 128], BF16), fS("WPB1", [128, NHC * 128], BF16)]

        rctx = ExitStack()
        rS = lambda n, shape, dt=F32: rctx.enter_context(nc.sbuf_tensor(f"{n}_{tag}", shape, dt))
        rP = lambda n, shape, dt=F32: rctx.enter_context(nc.psum_tensor(f"{n}_{tag}", shape, dt))
        XNBF = rS("XNBF", [128, NT * C], BF16)
        XNT = rS("XNT", [128, 2 * C], F32)
        XNTT = rS("XNTT", [128, 2 * C], F32)
        pTR = [rP("pTR0", [128, 512]), rP("pTR1", [128, 512])]
        pLG = [rP("pLG0", [128, E]), rP("pLG1", [128, E])]
        pCS = [rP("pCS0", [128, 16]), rP("pCS1", [128, 16])]
        pTOT = rP("pTOT", [128, 16])

        emit_ln(first=(call == 0))

        xnf_done, xnbf_done, trev_done, lgev_done = {}, {}, {}, {}
        tr_mm, lg_mm, cs_mm = {}, {}, {}
        idx_done, oh_done, csev_done, tots_ev = {}, {}, {}, {}

        for i in range(NT):
            @VEC
            def _(vector, i=i):
                if i >= 2:
                    vector.wait_ge(sPE.h, tr_mm[(i - 2, 1)])
                inst = vector.tensor_scalar(
                    out=XNT[:, (i % 2) * C:(i % 2 + 1) * C],
                    in0=X[:, i * C:(i + 1) * C],
                    scalar1=NEGM[:, i:i + 1], scalar2=RSTD[:, i:i + 1],
                    op0=ALU.add, op1=ALU.mult)
                sDV.inc(inst)
                xnf_done[i] = sDV.v
                vector.wait_ge(sDV.h, sDV.v)
                inst = vector.tensor_copy(out=XNBF[:, i * C:(i + 1) * C],
                                          in_=XNT[:, (i % 2) * C:(i % 2 + 1) * C])
                sDV.inc(inst)
                xnbf_done[i] = sDV.v

            for half in range(2):
                @TEN
                def _(tensor, i=i, half=half):
                    tensor.wait_ge(sDV.h, xnf_done[i])
                    k = i * 2 + half
                    if k >= 2:
                        tensor.wait_ge(sDV.h, trev_done[k - 2])
                    for j in range(4):
                        cc = half * 4 + j
                        inst = tensor.transpose(
                            pTR[half][:, j * 128:(j + 1) * 128],
                            XNT[:, (i % 2) * C + cc * 128:(i % 2) * C + (cc + 1) * 128],
                            IDF[:])
                    sPE.inc(inst)
                    tr_mm[(i, half)] = sPE.v

                @VEC
                def _(vector, i=i, half=half):
                    vector.wait_ge(sPE.h, tr_mm[(i, half)])
                    inst = vector.tensor_copy(
                        out=XNTT[:, (i % 2) * C + half * 512:(i % 2) * C + (half + 1) * 512],
                        in_=pTR[half][:])
                    sDV.inc(inst)
                    trev_done[i * 2 + half] = sDV.v

            @TEN
            def _(tensor, i=i):
                if i == 0 and call == 0:
                    tensor.wait_ge(sX.h, setup_loads)
                tensor.wait_ge(sDV.h, trev_done[i * 2 + 1])
                if i >= 2:
                    tensor.wait_ge(sDV.h, lgev_done[i - 2])
                pb = pLG[i % 2]
                for cc in range(NCC):
                    tensor.matmul(pb[:],
                                  XNTT[:, (i % 2) * C + cc * 128:(i % 2) * C + (cc + 1) * 128],
                                  RWC[:, cc * E:(cc + 1) * E],
                                  start=(cc == 0), stop=False)
                inst = tensor.matmul(pb[:], ONESF[0:1, :], RBR[0:1, :],
                                     start=False, stop=True)
                sPE.inc(inst)
                lg_mm[i] = sPE.v

            @VEC
            def _(vector, i=i):
                vector.wait_ge(sPE.h, lg_mm[i])
                inst = vector.tensor_copy(out=LG[:, (i % 2) * E:(i % 2 + 1) * E],
                                          in_=pLG[i % 2][:])
                sDV.inc(inst)
                lgev_done[i] = sDV.v
                vector.wait_ge(sDV.h, sDV.v)
                lg = LG[:, (i % 2) * E:(i % 2 + 1) * E]
                inst = vector.max(out=MX8[:], in_=lg)
                sDV.inc(inst)
                vector.wait_ge(sDV.h, sDV.v)
                inst = vector.max_index(out=IDXU[:], in_max=MX8[:], in_values=lg)
                sDV.inc(inst)
                vector.wait_ge(sDV.h, sDV.v)
                inst = vector.tensor_copy(out=EF[:], in_=IDXU[:, 0:2])
                sDV.inc(inst)
                vector.wait_ge(sDV.h, sDV.v)
                vector.tensor_tensor(out=DSUB[:, i:i + 1], in0=MX8[:, 0:1],
                                     in1=MX8[:, 1:2], op=ALU.subtract)
                vector.tensor_copy(out=ECOLS[:, 2 * i:2 * i + 2], in_=EF[:])
                vector.tensor_scalar(out=OH[:, i * 16:i * 16 + 8], in0=IOTAE[:],
                                     scalar1=EF[:, 0:1], scalar2=None, op0=ALU.is_equal)
                inst = vector.tensor_scalar(out=OH[:, i * 16 + 8:i * 16 + 16], in0=IOTAE[:],
                                            scalar1=EF[:, 1:2], scalar2=None, op0=ALU.is_equal)
                sDV.inc(inst)
                oh_done[i] = sDV.v
        all_oh = sDV.v

        @TEN
        def _(tensor):
            tensor.wait_ge(sDV.h, all_oh)
            for j in range(NT):
                inst = tensor.matmul(pTOT[:], ONESC[:], OH[:, j * 16:(j + 1) * 16],
                                     start=(j == 0), stop=(j == NT - 1))
            sPE.inc(inst)
            cs_mm["tots"] = sPE.v

        @VEC
        def _(vector):
            vector.wait_ge(sPE.h, cs_mm["tots"])
            inst = vector.tensor_copy(out=TOTS[:], in_=pTOT[:])
            sDV.inc(inst)
            tots_ev[0] = sDV.v

        gmark = {}
        for i in range(NT):
            @TEN
            def _(tensor, i=i):
                if i >= 2:
                    tensor.wait_ge(sDV.h, csev_done[i - 2])
                pb = pCS[i % 2]
                for j in range(i):
                    tensor.matmul(pb[:], ONESC[:], OH[:, j * 16:(j + 1) * 16],
                                  start=(j == 0), stop=False)
                inst = tensor.matmul(pb[:], TRI[:], OH[:, i * 16:(i + 1) * 16],
                                     start=(i == 0), stop=True)
                sPE.inc(inst)
                cs_mm[i] = sPE.v

            @VEC
            def _(vector, i=i):
                if i == 0:
                    vector.wait_ge(sDV.h, tots_ev[0])
                vector.wait_ge(sPE.h, cs_mm[i])
                inst = vector.tensor_copy(out=CST[:], in_=pCS[i % 2][:])
                sDV.inc(inst)
                csev_done[i] = sDV.v
                vector.wait_ge(sDV.h, sDV.v)

                def _step(ins):
                    sDV.inc(ins)
                    vector.wait_ge(sDV.h, sDV.v)
                _step(vector.tensor_copy(out=OHF[:], in_=OH[:, i * 16:(i + 1) * 16]))
                _step(vector.tensor_tensor(out=TMPA[:, 0:8], in0=OHF[:, 0:8],
                                           in1=CST[:, 0:8], op=ALU.mult))
                _step(vector.reduce_sum(out=POSC[:, 0:1], in_=TMPA[:, 0:8], axis=AX))
                _step(vector.tensor_tensor(out=TMPB[:, 0:8], in0=CST[:, 8:16],
                                           in1=TOTS[:, 0:8], op=ALU.add))
                _step(vector.tensor_tensor(out=TMPB[:, 8:16], in0=OHF[:, 8:16],
                                           in1=TMPB[:, 0:8], op=ALU.mult))
                _step(vector.reduce_sum(out=POSC[:, 1:2], in_=TMPB[:, 8:16], axis=AX))
                for k in range(2):
                    pos = POSC[:, k:k + 1]
                    _step(vector.tensor_scalar(out=POSC[:, 2 + k:3 + k],
                                               in0=ECOLS[:, 2 * i + k:2 * i + k + 1],
                                               scalar1=float(CAPP), scalar2=None, op0=ALU.mult))
                    _step(vector.tensor_scalar(out=POSC[:, 4 + k:5 + k], in0=pos,
                                               scalar1=613.5, scalar2=BIG,
                                               op0=ALU.is_gt, op1=ALU.mult))
                    _step(vector.tensor_scalar(out=POSC[:, 6 + k:7 + k], in0=pos,
                                               scalar1=613.0, scalar2=None, op0=ALU.min))
                    _step(vector.tensor_tensor(out=TMPA[:, 0:1], in0=POSC[:, 2 + k:3 + k],
                                               in1=pos, op=ALU.add))
                    _step(vector.tensor_tensor(out=TMPA[:, 1:2], in0=TMPA[:, 0:1],
                                               in1=POSC[:, 4 + k:5 + k], op=ALU.add))
                    _step(vector.tensor_scalar(out=TMPA[:, 1:2], in0=TMPA[:, 1:2],
                                               scalar1=float(E * CAPP), scalar2=None,
                                               op0=ALU.min))
                    _step(vector.tensor_copy(out=SIDX[:, 2 * i + k:2 * i + k + 1],
                                             in_=TMPA[:, 1:2]))
                    _step(vector.tensor_tensor(out=TMPA[:, 2:3], in0=POSC[:, 2 + k:3 + k],
                                               in1=POSC[:, 6 + k:7 + k], op=ALU.add))
                    _step(vector.tensor_copy(out=RIDX[:, 2 * i + k:2 * i + k + 1],
                                             in_=TMPA[:, 2:3]))
                idx_done[i] = sDV.v

            @GPS
            def _(gpsimd, i=i):
                gpsimd.wait_ge(sDV.h, idx_done[i])
                gpsimd.wait_ge(sDV.h, xnbf_done[i])
                for k in range(2):
                    inst = gpsimd.indirect_dma_start(
                        out=dxrows,
                        out_offset=IndirectOffsetOnAxis(SIDX[:, 2 * i + k:2 * i + k + 1], 0),
                        in_=XNBF[:, i * C:(i + 1) * C], in_offset=None)
                    sSC.inc(inst, 16)
        scatter_done = sSC.v
        all_idx = sDV.v

        @SCA
        def _(scalar):
            scalar.wait_ge(sDV.h, all_idx)
            inst = scalar.activation(out=P0T[:], in_=DSUB[:], func=AF.Sigmoid)
            sAC.inc(inst)
        sig_done = sAC.v

        @VEC
        def _(vector):
            vector.wait_ge(sAC.h, sig_done)
            vector.tensor_scalar(out=HP0[:], in0=P0T[:], scalar1=0.5, scalar2=None,
                                 op0=ALU.mult)
            inst = vector.tensor_scalar(out=HP1[:], in0=P0T[:], scalar1=-0.5,
                                        scalar2=0.5, op0=ALU.mult, op1=ALU.add)
            sDV.inc(inst)
        hp_done = sDV.v

        rctx.close()

        pctx = ExitStack()
        pS = lambda n, shape, dt=F32: pctx.enter_context(nc.sbuf_tensor(f"{n}_{tag}", shape, dt))
        pP = lambda n, shape, dt=F32: pctx.enter_context(nc.psum_tensor(f"{n}_{tag}", shape, dt))
        EINPT = [pS("EINPT0", [128, NCC * CAPP], BF16),
                 pS("EINPT1", [128, NCC * CAPP], BF16)]
        MSIL = pS("MSIL", [128, NHC * CAP], BF16)
        HSB = [pS("HSB0", [128, 512], F32), pS("HSB1", [128, 512], F32)]
        MSB = [pS("MSB0", [128, 512], F32), pS("MSB1", [128, 512], F32)]
        OSB = pS("OSB", [128, NCC * CAP], BF16)
        OROW = [pS("OROW0", [128, C], BF16), pS("OROW1", [128, C], BF16)]
        SEL = [pS("SEL0", [128, C], BF16), pS("SEL1", [128, C], BF16)]
        pH = [pP("pH0", [128, 512]), pP("pH1", [128, 512])]
        pG = [pP("pG0", [128, 512]), pP("pG1", [128, 512])]
        pPj = [pP("pPj0", [128, 512]), pP("pPj1", [128, 512])]
        pOT = pP("pOT", [128, 1024], BF16)

        wfc_mark, wp_mark, wfc_used, wp_used = {}, {}, {}, {}
        er_mark, einp_used = {}, {}
        hg_mm, h_ev, g_mu, sil_mk = {}, {}, {}, {}
        pj_mm, o_ev, otr_mm, orow_ev = {}, {}, {}, {}
        silu_last, proj_last, otr_last = {}, {}, {}
        orow_st_list = []

        for e in range(E):
            @SYN
            def _(sync, e=e):
                if e == 0:
                    sync.wait_ge(sSC.h, scatter_done)
                if e >= 2:
                    sync.wait_ge(sPE.h, einp_used[e - 2])
                ebuf = EINPT[e % 2]
                for cc in range(NCC):
                    for s0, sn in ((0, 512), (512, 128)):
                        inst = sync.dma_start(
                            out=ebuf[:, cc * CAPP + s0:cc * CAPP + s0 + sn],
                            in_=dxrows[e * CAPP + s0:e * CAPP + s0 + sn,
                                       cc * 128:(cc + 1) * 128],
                            transpose=True)
                        sER.inc(inst, 16)
                er_mark[e] = sER.v

            for hg in range(NHG):
                @GPS
                def _(gpsimd, e=e, hg=hg):
                    gidx = e * NHG + hg
                    if gidx >= 2:
                        gpsimd.wait_ge(sPE.h, wfc_used[gidx - 2])
                    hsl = slice(hg * HGRP, (hg + 1) * HGRP)
                    buf = gidx % 2
                    inst = gpsimd.dma_start(
                        out=WFCB[buf][:].rearrange("p (cc h) -> p cc h", cc=NCC),
                        in_=d["wfc"][e, :, hsl].rearrange("(cc p) h -> p cc h", p=128))
                    sW.inc(inst, 16)
                    inst = gpsimd.dma_start(
                        out=WGB[buf][:].rearrange("p (cc h) -> p cc h", cc=NCC),
                        in_=d["wg"][e, :, hsl].rearrange("(cc p) h -> p cc h", p=128))
                    sW.inc(inst, 16)
                    wfc_mark[e * NHG + hg] = sW.v

                for hsub in range(HGRP // 128):
                    hc = hg * (HGRP // 128) + hsub
                    for gi, (s0, sn) in enumerate(SGRPS):
                        q = hc * 2 + gi

                        @TEN
                        def _(tensor, e=e, hg=hg, hc=hc, gi=gi, s0=s0, sn=sn, q=q):
                            ebuf = EINPT[e % 2]
                            buf = (e * NHG + hg) % 2
                            pi = q % 2
                            if q == 0:
                                tensor.wait_ge(sER.h, er_mark[e])
                            if gi == 0 and hc % (HGRP // 128) == 0:
                                tensor.wait_ge(sW.h, wfc_mark[e * NHG + hg])
                            if q >= 2:
                                wk = (e, q - 2)
                            elif e > 0:
                                wk = (e - 1, 2 * NHC - 2 + q)
                            else:
                                wk = None
                            if wk is not None:
                                tensor.wait_ge(sAC.h, h_ev[wk])
                                tensor.wait_ge(sDV.h, g_mu[wk])
                            hof = hc % (HGRP // 128)
                            for cc in range(NCC):
                                tensor.matmul(
                                    pH[pi][:, 0:sn],
                                    WFCB[buf][:, cc * HGRP + hof * 128:cc * HGRP + (hof + 1) * 128],
                                    ebuf[:, cc * CAPP + s0:cc * CAPP + s0 + sn],
                                    start=(cc == 0), stop=(cc == NCC - 1))
                            for cc in range(NCC):
                                inst = tensor.matmul(
                                    pG[pi][:, 0:sn],
                                    WGB[buf][:, cc * HGRP + hof * 128:cc * HGRP + (hof + 1) * 128],
                                    ebuf[:, cc * CAPP + s0:cc * CAPP + s0 + sn],
                                    start=(cc == 0), stop=(cc == NCC - 1))
                            sPE.inc(inst)
                            hg_mm[(e, q)] = sPE.v
                            if gi == 1 and hof == (HGRP // 128) - 1:
                                wfc_used[e * NHG + hg] = sPE.v
                            if q == 2 * NHC - 1:
                                einp_used[e] = sPE.v

                        @SCA
                        def _(scalar, e=e, hc=hc, s0=s0, sn=sn, q=q):
                            pi = q % 2
                            scalar.wait_ge(sPE.h, hg_mm[(e, q)])
                            inst = scalar.activation(
                                out=HSB[pi][:, 0:sn], in_=pH[pi][:, 0:sn], func=AF.Identity,
                                bias=BFCC[:, e * NHC + hc:e * NHC + hc + 1], scale=1.0)
                            sAC.inc(inst)
                            h_ev[(e, q)] = sAC.v

                        @VEC
                        def _(vector, e=e, hc=hc, s0=s0, sn=sn, q=q):
                            pi = q % 2
                            vector.wait_ge(sAC.h, h_ev[(e, q)])
                            inst = vector.scalar_tensor_tensor(
                                out=MSB[pi][:, 0:sn], in0=pG[pi][:, 0:sn],
                                scalar=BGC[:, e * NHC + hc:e * NHC + hc + 1],
                                in1=HSB[pi][:, 0:sn], op0=ALU.add, op1=ALU.mult)
                            sDV.inc(inst)
                            g_mu[(e, q)] = sDV.v

                        @SCA
                        def _(scalar, e=e, hc=hc, s0=s0, sn=sn, q=q):
                            pi = q % 2
                            scalar.wait_ge(sDV.h, g_mu[(e, q)])
                            if e > 0 and q == 0:
                                scalar.wait_ge(sPE.h, proj_last[e - 1])
                            inst = scalar.activation(
                                out=MSIL[:, hc * CAP + s0:hc * CAP + s0 + sn],
                                in_=MSB[pi][:, 0:sn], func=AF.Silu)
                            sAC.inc(inst)
                            sil_mk[(e, q)] = sAC.v
                            if q == 2 * NHC - 1:
                                silu_last[e] = sAC.v

            for cc in range(NCC):
                @GPS
                def _(gpsimd, e=e, cc=cc):
                    cidx = e * NCC + cc
                    if cidx >= 2:
                        gpsimd.wait_ge(sPE.h, wp_used[cidx - 2])
                    csl = slice(cc * 128, (cc + 1) * 128)
                    inst = gpsimd.dma_start(
                        out=WPB[cidx % 2][:].rearrange("p (hc c) -> p hc c", hc=NHC),
                        in_=d["wp"][e, :, csl].rearrange("(hc p) c -> p hc c", p=128))
                    sW.inc(inst, 16)
                    wp_mark[cidx] = sW.v

                for gi, (s0, sn) in enumerate(SGRPS):
                    q = cc * 2 + gi

                    @TEN
                    def _(tensor, e=e, cc=cc, gi=gi, s0=s0, sn=sn, q=q):
                        if q == 0:
                            tensor.wait_ge(sAC.h, silu_last[e])
                        if gi == 0:
                            tensor.wait_ge(sW.h, wp_mark[e * NCC + cc])
                        if q >= 2:
                            tensor.wait_ge(sDV.h, o_ev[(e, q - 2)])
                        elif e > 0:
                            tensor.wait_ge(sDV.h, o_ev[(e - 1, 2 * NCC - 2 + q)])
                        wb = WPB[(e * NCC + cc) % 2]
                        pi = q % 2
                        for hcj in range(NHC):
                            inst = tensor.matmul(
                                pPj[pi][:, 0:sn], wb[:, hcj * 128:(hcj + 1) * 128],
                                MSIL[:, hcj * CAP + s0:hcj * CAP + s0 + sn],
                                start=(hcj == 0), stop=(hcj == NHC - 1))
                        sPE.inc(inst)
                        pj_mm[(e, q)] = sPE.v
                        if gi == 1:
                            wp_used[e * NCC + cc] = sPE.v
                        if q == 2 * NCC - 1:
                            proj_last[e] = sPE.v

                    @VEC
                    def _(vector, e=e, cc=cc, s0=s0, sn=sn, q=q):
                        vector.wait_ge(sPE.h, pj_mm[(e, q)])
                        if e > 0 and q == 0:
                            vector.wait_ge(sPE.h, otr_last[e - 1])
                        pi = q % 2
                        inst = vector.tensor_scalar(
                            out=OSB[:, cc * CAP + s0:cc * CAP + s0 + sn],
                            in0=pPj[pi][:, 0:sn],
                            scalar1=BPC[:, e * NCC + cc:e * NCC + cc + 1],
                            scalar2=None, op0=ALU.add)
                        sDV.inc(inst)
                        o_ev[(e, q)] = sDV.v

            for sc_i in range(5):
                s0 = sc_i * 128
                sn = min(128, CAP - s0)

                @TEN
                def _(tensor, e=e, sc_i=sc_i, s0=s0, sn=sn):
                    tensor.wait_ge(sDV.h, o_ev[(e, 2 * NCC - 1)])
                    if sc_i >= 1:
                        tensor.wait_ge(sDV.h, orow_ev[(e, sc_i - 1)])
                    elif e > 0:
                        tensor.wait_ge(sDV.h, orow_ev[(e - 1, 4)])
                    for cc in range(NCC):
                        inst = tensor.transpose(
                            pOT[0:sn, cc * 128:(cc + 1) * 128],
                            OSB[:, cc * CAP + s0:cc * CAP + s0 + sn], IDB[:])
                    sPE.inc(inst)
                    otr_mm[(e, sc_i)] = sPE.v
                    if sc_i == 4:
                        otr_last[e] = sPE.v

                @VEC
                def _(vector, e=e, sc_i=sc_i, sn=sn):
                    vector.wait_ge(sPE.h, otr_mm[(e, sc_i)])
                    if len(orow_st_list) >= 2:
                        vector.wait_ge(sOR.h, orow_st_list[-2])
                    inst = vector.tensor_copy(out=OROW[sc_i % 2][0:sn, :], in_=pOT[0:sn, :])
                    sDV.inc(inst)
                    orow_ev[(e, sc_i)] = sDV.v

                @SCA
                def _(scalar, e=e, sc_i=sc_i, s0=s0, sn=sn):
                    scalar.wait_ge(sDV.h, orow_ev[(e, sc_i)])
                    inst = scalar.dma_start(
                        out=dorows[e * CAPP + s0:e * CAPP + s0 + sn, :],
                        in_=OROW[sc_i % 2][0:sn, :])
                    sOR.inc(inst, 16)
                orow_st_list.append(sOR.v)
        or_all = sOR.v

        comb = {}
        for i in range(NT):
            @GPS
            def _(gpsimd, i=i):
                if i == 0:
                    gpsimd.wait_ge(sOR.h, or_all)
                if i >= 1:
                    gpsimd.wait_ge(sDV.h, comb[i - 1])
                for k in range(2):
                    inst = gpsimd.indirect_dma_start(
                        out=SEL[k][:], out_offset=None,
                        in_=dorows,
                        in_offset=IndirectOffsetOnAxis(RIDX[:, 2 * i + k:2 * i + k + 1], 0))
                    sGA.inc(inst, 16)
            gmark[i] = sGA.v

            @VEC
            def _(vector, i=i):
                if i == 0:
                    vector.wait_ge(sDV.h, hp_done)
                vector.wait_ge(sGA.h, gmark[i])
                xt = X[:, i * C:(i + 1) * C]
                inst = vector.scalar_tensor_tensor(
                    out=xt, in0=SEL[0][:], scalar=HP0[:, i:i + 1], in1=xt,
                    op0=ALU.mult, op1=ALU.add)
                sDV.inc(inst)
                vector.wait_ge(sDV.h, sDV.v)
                inst = vector.scalar_tensor_tensor(
                    out=xt, in0=SEL[1][:], scalar=HP1[:, i:i + 1], in1=xt,
                    op0=ALU.mult, op1=ALU.add)
                sDV.inc(inst)
                comb[i] = sDV.v
        pctx.close()
        fctx.close()
        return comb[NT - 1]

    def emit_attn():
        actx = ExitStack()
        aS = lambda n, shape, dt=F32: actx.enter_context(nc.sbuf_tensor(n, shape, dt))
        aP = lambda n, shape, dt=F32: actx.enter_context(nc.psum_tensor(n, shape, dt))

        XNBF2 = aS("XNBF2", [128, C], BF16)
        WINT = aS("WINT", [128, NCC * 2 * C], BF16)
        WOUTT = aS("WOUTT", [128, 16 * C], BF16)
        RCB = aS("RCB", [128, 4 * 256], BF16)
        RSB = aS("RSB", [128, 4 * 256], BF16)
        RICB = aS("RICB", [128, 2 * 256], BF16)
        RISB = aS("RISB", [128, 2 * 256], BF16)
        BINR = aS("BINR", [1, 2 * C], BF16)
        SPEC = aS("SPEC", [128, 4 * 2 * C], BF16)
        ZBUF = aS("ZBUF", [128, 4 * 2 * C], BF16)
        ZS = aS("ZS", [128, 4 * 512], BF16)
        YU = aS("YU", [128, 2 * 1024], BF16)
        YV = aS("YV", [128, 2 * 1024], BF16)
        TMW = aS("TMW", [128, 512], BF16)
        TMG = aS("TMG", [128, 512], BF16)
        XN2T = aS("XN2T", [128, C], BF16)
        CASB = aS("CASB", [128, 256], F32)
        CBSB = aS("CBSB", [128, 256], F32)
        SGA = aS("SGA", [128, 256], F32)
        SGB = aS("SGB", [128, 256], F32)
        GATED = aS("GATED", [128, 16 * 256], BF16)
        ATSB = aS("ATSB", [128, 256], F32)
        BACOL = aS("BACOL", [128, NCC], F32)
        BBCOL = aS("BBCOL", [128, NCC], F32)
        BOCOL = aS("BOCOL", [128, NCC], F32)

        pZ = aP("pZ", [128, 512])
        pXT = aP("pXT", [128, 1024], BF16)
        pFW = [aP("pFW0", [128, 512]), aP("pFW1", [128, 512])]
        pIV = [aP("pIV0", [128, 256]), aP("pIV1", [128, 256])]
        pOP = aP("pOP", [128, 256])
        pAT = aP("pAT", [128, 256])

        @SYN
        def _(sync):
            for dst, name in [(RICB, "ric"), (RISB, "ris")]:
                inst = sync.dma_start(
                    out=dst[:].rearrange("p (f t) -> p f t", f=2),
                    in_=d[name].rearrange("(f p) t -> p f t", p=128))
                sX.inc(inst, 16)
            for dst, name in [(BACOL, "b_a_col"), (BBCOL, "b_b_col"), (BOCOL, "b_out_col")]:
                inst = sync.dma_start(out=dst[:], in_=d[name][:])
                sX.inc(inst, 16)
            inst = sync.dma_start(
                out=RCB[:].rearrange("p (tc f) -> p tc f", tc=4),
                in_=d["rc"].rearrange("(tc p) f -> p tc f", p=128))
            sX.inc(inst, 16)
            inst = sync.dma_start(
                out=RSB[:].rearrange("p (tc f) -> p tc f", tc=4),
                in_=d["rs"].rearrange("(tc p) f -> p tc f", p=128))
            sX.inc(inst, 16)
        aconst = sX.v

        WREV = GATED[:, 0:2 * C]

        @GPS
        def _(gpsimd):
            inst = gpsimd.dma_start(
                out=WINT[:].rearrange("p (hc c) -> p hc c", hc=NCC),
                in_=d["w_inT"].rearrange("(hc p) c -> p hc c", p=128))
            sW.inc(inst, 16)
            inst = gpsimd.dma_start(
                out=WOUTT[:].rearrange("p (ec h) -> p ec h", ec=16),
                in_=d["w_outT"].rearrange("(ec p) h -> p ec h", p=128))
            sW.inc(inst, 16)
            inst = gpsimd.dma_start(out=BINR[:], in_=d["b_in_row"][:])
            sW.inc(inst, 16)
            inst = gpsimd.memset(ZBUF[:, 0:2 * 2 * C], 0.0)
            sPO.inc(inst)
        aw_done = sW.v
        zhist = sPO.v

        spec_mm, spec_ev, wrev_mark = {}, {}, {}
        fwd_mm, fwd_ev, yu_mk, yv_mk = {}, {}, {}, {}
        iv_mm, ca_ev, sg_mk, gat_mk = {}, {}, {}, {}
        op_mm, op_ev, at_mm, xadd = {}, {}, {}, {}
        xnb2_done, xtr_mm, xtr_ev, z_mm, z_ev = {}, {}, {}, {}, {}
        fwd_done_w = {}
        roll = {"fwbank": [None, None], "yfree": None, "slice": {}}

        sbank = [0]
        for filt in range(2):
            @GPS
            def _(gpsimd, filt=filt):
                if filt == 1:
                    gpsimd.wait_ge(sPE.h, spec_mm[(0, "last")])
                inst = gpsimd.dma_start(
                    out=WREV.rearrange("p (kc c) -> p kc c", kc=2),
                    in_=d["wab_rev"][filt].rearrange("(kc p) c -> p kc c", p=128))
                sW.inc(inst, 16)
                wrev_mark[filt] = sW.v

            for uv in range(2):
                for fc in range(2):
                    for cg in range(4):
                        bk = sbank[0] % 2
                        sbank[0] += 1

                        @TEN
                        def _(tensor, filt=filt, uv=uv, fc=fc, cg=cg, bk=bk):
                            if uv == 0 and fc == 0 and cg == 0:
                                tensor.wait_ge(sW.h, wrev_mark[filt])
                                if filt == 0:
                                    tensor.wait_ge(sX.h, aconst)
                            km = roll["fwbank"][bk]
                            if km is not None:
                                tensor.wait_ge(sDV.h, km)
                            RB = RCB if uv == 0 else RSB
                            for kc in range(2):
                                inst = tensor.matmul(
                                    pFW[bk][:, 0:256],
                                    RB[:, kc * 256 + fc * 128:kc * 256 + (fc + 1) * 128],
                                    WREV[:, kc * C + cg * 256:kc * C + (cg + 1) * 256],
                                    start=(kc == 0), stop=(kc == 1))
                            sPE.inc(inst)
                            spec_mm[(filt, uv, fc, cg)] = sPE.v
                            spec_mm[(filt, "last")] = sPE.v

                        @VEC
                        def _(vector, filt=filt, uv=uv, fc=fc, cg=cg, bk=bk):
                            vector.wait_ge(sPE.h, spec_mm[(filt, uv, fc, cg)])
                            col = (uv * 2 + fc) * 2 * C + filt * C + cg * 256
                            inst = vector.tensor_copy(out=SPEC[:, col:col + 256],
                                                      in_=pFW[bk][:, 0:256])
                            sDV.inc(inst)
                            spec_ev[(filt, uv, fc, cg)] = sDV.v
                            roll["fwbank"][bk] = sDV.v
        spec_all = sDV.v

        emit_ln(first=False)

        def emit_ztile(i):
            @VEC
            def _(vector, i=i):
                if i >= 1:
                    vector.wait_ge(sPE.h, xtr_mm[i - 1])
                inst = vector.tensor_scalar(
                    out=XNBF2[:],
                    in0=X[:, i * C:(i + 1) * C],
                    scalar1=NEGM[:, i:i + 1], scalar2=RSTD[:, i:i + 1],
                    op0=ALU.add, op1=ALU.mult)
                sDV.inc(inst)
                xnb2_done[i] = sDV.v

            @TEN
            def _(tensor, i=i):
                tensor.wait_ge(sDV.h, xnb2_done[i])
                if i >= 1:
                    tensor.wait_ge(sAC.h, xtr_ev[i - 1])
                for j in range(NCC):
                    inst = tensor.transpose(
                        pXT[:, j * 128:(j + 1) * 128],
                        XNBF2[:, j * 128:(j + 1) * 128],
                        IDB[:])
                sPE.inc(inst)
                xtr_mm[i] = sPE.v

            @SCA
            def _(scalar, i=i):
                scalar.wait_ge(sPE.h, xtr_mm[i])
                if i >= 1:
                    scalar.wait_ge(sPE.h, z_mm[(i - 1, 3)])
                inst = scalar.activation(out=XN2T[:], in_=pXT[:], func=AF.Copy)
                sAC.inc(inst)
                xtr_ev[i] = sAC.v

            for cg in range(4):
                @TEN
                def _(tensor, i=i, cg=cg):
                    if i == 0 and cg == 0:
                        tensor.wait_ge(sW.h, aw_done)
                    tensor.wait_ge(sAC.h, xtr_ev[i])
                    if (i, cg) != (0, 0):
                        pi, pc = (i, cg - 1) if cg else (i - 1, 3)
                        tensor.wait_ge(sAC.h, z_ev[(pi, pc)])
                    for hc in range(NCC):
                        tensor.matmul(pZ[:], XN2T[:, hc * 128:(hc + 1) * 128],
                                      WINT[:, hc * 2 * C + cg * 512:hc * 2 * C + (cg + 1) * 512],
                                      start=(hc == 0), stop=False)
                    inst = tensor.matmul(pZ[:], ONESB[0:1, :],
                                         BINR[0:1, cg * 512:(cg + 1) * 512],
                                         start=False, stop=True)
                    sPE.inc(inst)
                    z_mm[(i, cg)] = sPE.v

                @SCA
                def _(scalar, i=i, cg=cg):
                    scalar.wait_ge(sPE.h, z_mm[(i, cg)])
                    if i == 0 and cg == 0:
                        scalar.wait_ge(sPO.h, zhist)
                    if i >= 4 and cg == 0:
                        scalar.wait_ge(sPE.h, fwd_done_w[(i - 2) // 2])
                    slot = (i + 2) % 4
                    inst = scalar.activation(
                        out=ZBUF[:, slot * 2 * C + cg * 512:slot * 2 * C + (cg + 1) * 512],
                        in_=pZ[:], func=AF.Copy)
                    sAC.inc(inst)
                    z_ev[(i, cg)] = sAC.v

        def emit_window(w):
            fbank = [0]
            for half in range(2):
                for fc in range(2):
                    for cg in range(2):
                        ch0 = [0, 1024][cg] + half * 512
                        for uv in range(2):
                            bk = fbank[0] % 2
                            fbank[0] += 1

                            @TEN
                            def _(tensor, w=w, half=half, fc=fc, cg=cg, uv=uv, ch0=ch0, bk=bk):
                                if half == 0 and fc == 0 and cg == 0 and uv == 0:
                                    tensor.wait_ge(sAC.h, z_ev[(2 * w + 1, 3)])
                                    if w == 0:
                                        tensor.wait_ge(sDV.h, spec_all)
                                    ym = roll["yfree"]
                                    if ym is not None:
                                        tensor.wait_ge(sPE.h, ym)
                                km = roll["fwbank"][bk]
                                if km is not None:
                                    tensor.wait_ge(sDV.h, km)
                                RB = RCB if uv == 0 else RSB
                                for tj in range(4):
                                    j = 2 * w - 2 + tj
                                    slot = (j + 2) % 4
                                    inst = tensor.matmul(
                                        pFW[bk][:],
                                        RB[:, tj * 256 + fc * 128:tj * 256 + (fc + 1) * 128],
                                        ZBUF[:, slot * 2 * C + ch0:slot * 2 * C + ch0 + 512],
                                        start=(tj == 0), stop=(tj == 3))
                                sPE.inc(inst)
                                fwd_mm[(w, half, fc, cg, uv)] = sPE.v

                            @VEC
                            def _(vector, w=w, half=half, fc=fc, cg=cg, uv=uv, bk=bk):
                                vector.wait_ge(sPE.h, fwd_mm[(w, half, fc, cg, uv)])
                                sl = roll["slice"].get((uv, cg))
                                if sl is not None:
                                    vector.wait_ge(sPO.h, sl)
                                inst = vector.tensor_copy(
                                    out=ZS[:, (uv * 2 + cg) * 512:(uv * 2 + cg) * 512 + 512],
                                    in_=pFW[bk][:])
                                sDV.inc(inst)
                                vector.wait_ge(sDV.h, sDV.v)
                                fwd_ev[(w, half, fc, cg, uv)] = sDV.v
                                roll["fwbank"][bk] = sDV.v

                        zu = ZS[:, (0 * 2 + cg) * 512:(0 * 2 + cg) * 512 + 512]
                        zv = ZS[:, (1 * 2 + cg) * 512:(1 * 2 + cg) * 512 + 512]
                        wu = SPEC[:, (0 * 2 + fc) * 2 * C + ch0:(0 * 2 + fc) * 2 * C + ch0 + 512]
                        wv = SPEC[:, (1 * 2 + fc) * 2 * C + ch0:(1 * 2 + fc) * 2 * C + ch0 + 512]
                        yuc = YU[:, fc * 1024 + cg * 512:fc * 1024 + (cg + 1) * 512]
                        yvc = YV[:, fc * 1024 + cg * 512:fc * 1024 + (cg + 1) * 512]

                        @VEC
                        def _(vector, w=w, half=half, fc=fc, cg=cg,
                              zu=zu, zv=zv, wu=wu, wv=wv, yuc=yuc):
                            inst = vector.tensor_tensor(out=yuc, in0=zu, in1=wu, op=ALU.mult)
                            sDV.inc(inst)
                            inst = vector.tensor_tensor(out=TMW[:], in0=zv, in1=wv, op=ALU.mult)
                            sDV.inc(inst)
                            vector.wait_ge(sDV.h, sDV.v)
                            inst = vector.tensor_tensor(out=yuc, in0=yuc,
                                                        in1=TMW[:], op=ALU.subtract)
                            sDV.inc(inst)
                            if fc == 0:
                                vector.wait_ge(sDV.h, sDV.v)
                                inst = vector.tensor_tensor(out=yuc[0:1, :], in0=zu[0:1, :],
                                                            in1=wu[0:1, :], op=ALU.mult)
                                sDV.inc(inst)
                            yu_mk[(w, half, fc, cg)] = sDV.v

                        @GPS
                        def _(gpsimd, w=w, half=half, fc=fc, cg=cg,
                              zu=zu, zv=zv, wu=wu, wv=wv, yvc=yvc):
                            gpsimd.wait_ge(sDV.h, fwd_ev[(w, half, fc, cg, 1)])
                            inst = gpsimd.tensor_tensor(out=yvc, in0=zu, in1=wv, op=ALU.mult)
                            sPO.inc(inst)
                            inst = gpsimd.tensor_tensor(out=TMG[:], in0=zv, in1=wu, op=ALU.mult)
                            sPO.inc(inst)
                            gpsimd.wait_ge(sPO.h, sPO.v)
                            inst = gpsimd.tensor_tensor(out=yvc, in0=yvc, in1=TMG[:], op=ALU.add)
                            sPO.inc(inst)
                            if fc == 0:
                                gpsimd.wait_ge(sPO.h, sPO.v)
                                inst = gpsimd.tensor_tensor(out=yvc[0:1, :], in0=zv[0:1, :],
                                                            in1=wv[0:1, :], op=ALU.mult)
                                sPO.inc(inst)
                            yv_mk[(w, half, fc, cg)] = sPO.v
                            roll["slice"][(0, cg)] = sPO.v
                            roll["slice"][(1, cg)] = sPO.v

                for cil in range(4):
                    ci = half * 4 + cil
                    for ab in range(2):
                        loc = cil if ab == 0 else 4 + cil

                        @TEN
                        def _(tensor, w=w, half=half, ci=ci, ab=ab, loc=loc, cil=cil):
                            if cil == 0 and ab == 0:
                                for fc in range(2):
                                    for cg in range(2):
                                        tensor.wait_ge(sDV.h, yu_mk[(w, half, fc, cg)])
                                        tensor.wait_ge(sPO.h, yv_mk[(w, half, fc, cg)])
                            km = ca_ev.get(("bank", ab))
                            if km:
                                tensor.wait_ge(sDV.h, km)
                            pb = pIV[ab]
                            for fc in range(2):
                                tensor.matmul(pb[:],
                                              YU[:, fc * 1024 + loc * 128:fc * 1024 + (loc + 1) * 128],
                                              RICB[:, fc * 256:(fc + 1) * 256],
                                              start=(fc == 0), stop=False)
                            for fc in range(2):
                                inst = tensor.matmul(pb[:],
                                                     YV[:, fc * 1024 + loc * 128:fc * 1024 + (loc + 1) * 128],
                                                     RISB[:, fc * 256:(fc + 1) * 256],
                                                     start=False, stop=(fc == 1))
                            sPE.inc(inst)
                            iv_mm[(w, ci, ab)] = sPE.v
                            if cil == 3 and ab == 1:
                                roll["yfree"] = sPE.v

                        @VEC
                        def _(vector, w=w, ci=ci, ab=ab):
                            vector.wait_ge(sPE.h, iv_mm[(w, ci, ab)])
                            bias = (BACOL if ab == 0 else BBCOL)[:, ci:ci + 1]
                            dst = CASB if ab == 0 else CBSB
                            inst = vector.tensor_scalar(out=dst[:], in0=pIV[ab][:],
                                                        scalar1=bias, scalar2=None, op0=ALU.add)
                            sDV.inc(inst)
                            ca_ev[(w, ci, ab)] = sDV.v
                            ca_ev[("bank", ab)] = sDV.v

                    @SCA
                    def _(scalar, w=w, ci=ci):
                        scalar.wait_ge(sDV.h, ca_ev[(w, ci, 0)])
                        inst = scalar.activation(out=SGA[:], in_=CASB[:], func=AF.Sigmoid)
                        sAC.inc(inst)
                        scalar.wait_ge(sDV.h, ca_ev[(w, ci, 1)])
                        inst = scalar.activation(out=SGB[:], in_=CBSB[:], func=AF.Sigmoid)
                        sAC.inc(inst)
                        sg_mk[(w, ci)] = sAC.v

                    @VEC
                    def _(vector, w=w, ci=ci):
                        vector.wait_ge(sAC.h, sg_mk[(w, ci)])
                        if ci == 0:
                            km = op_mm.get(("gated", w - 1))
                            if km:
                                vector.wait_ge(sPE.h, km)
                        vector.tensor_tensor(out=GATED[:, ci * 256:(ci + 1) * 256],
                                             in0=CASB[:], in1=SGB[:], op=ALU.mult)
                        inst = vector.tensor_tensor(out=GATED[:, (8 + ci) * 256:(9 + ci) * 256],
                                                    in0=CBSB[:], in1=SGA[:], op=ALU.mult)
                        sDV.inc(inst)
                        gat_mk[(w, ci)] = sDV.v

            for hc in range(NCC):
                @TEN
                def _(tensor, w=w, hc=hc):
                    if hc == 0:
                        if w == 0:
                            tensor.wait_ge(sW.h, aw_done)
                        for ci in range(8):
                            tensor.wait_ge(sDV.h, gat_mk[(w, ci)])
                    km = op_ev.get(("bank",))
                    if km:
                        tensor.wait_ge(sAC.h, km)
                    for ec in range(16):
                        inst = tensor.matmul(pOP[:],
                                             WOUTT[:, ec * C + hc * 128:ec * C + (hc + 1) * 128],
                                             GATED[:, ec * 256:(ec + 1) * 256],
                                             start=(ec == 0), stop=(ec == 15))
                    sPE.inc(inst)
                    op_mm[(w, hc)] = sPE.v
                    if hc == NCC - 1:
                        op_mm[("gated", w)] = sPE.v

                @SCA
                def _(scalar, w=w, hc=hc):
                    scalar.wait_ge(sPE.h, op_mm[(w, hc)])
                    km = at_mm.get(("atsb",))
                    if km:
                        scalar.wait_ge(sPE.h, km)
                    inst = scalar.activation(out=ATSB[:], in_=pOP[:], func=AF.Identity,
                                             bias=BOCOL[:, hc:hc + 1], scale=1.0)
                    sAC.inc(inst)
                    op_ev[(w, hc)] = sAC.v
                    op_ev[("bank",)] = sAC.v

                @TEN
                def _(tensor, w=w, hc=hc):
                    tensor.wait_ge(sAC.h, op_ev[(w, hc)])
                    km = xadd.get(("bank",))
                    if km:
                        tensor.wait_ge(sDV.h, km)
                    for th in range(2):
                        inst = tensor.transpose(pAT[:, th * 128:(th + 1) * 128],
                                                ATSB[:, th * 128:(th + 1) * 128], IDF[:])
                    sPE.inc(inst)
                    at_mm[(w, hc)] = sPE.v
                    at_mm[("atsb",)] = sPE.v

                @VEC
                def _(vector, w=w, hc=hc):
                    vector.wait_ge(sPE.h, at_mm[(w, hc)])
                    for th in range(2):
                        i = 2 * w + th
                        xs = X[:, i * C + hc * 128:i * C + (hc + 1) * 128]
                        inst = vector.tensor_tensor(out=xs,
                                                    in0=pAT[:, th * 128:(th + 1) * 128],
                                                    in1=xs, op=ALU.add)
                        sDV.inc(inst)
                    xadd[(w, hc)] = sDV.v
                    xadd[("bank",)] = sDV.v

        for i in range(NT):
            emit_ztile(i)
            if i % 2 == 1:
                w = (i - 1) // 2
                emit_window(w)
                fwd_done_w[w] = max(v for k, v in fwd_mm.items() if k[0] == w)
        actx.close()

    emit_ffn(0)
    barrier()
    emit_attn()
    barrier()
    last_comb = emit_ffn(1)

    @SYN
    def _(sync):
        sync.wait_ge(sDV.h, last_comb)
        inst = sync.dma_start(out=dout.rearrange("(i p) c -> p i c", p=128),
                              in_=X[:].rearrange("p (i c) -> p i c", i=NT))
        sST.inc(inst, 16)
        sync.wait_ge(sST.h, sST.v)

    blk.__exit__(None, None, None)
    ctx.close()
    return nc


def _dft_consts():
    t = np.arange(NFFT)
    f = np.arange(256)
    ang = 2 * np.pi * np.outer(t, f) / NFFT
    rc = np.cos(ang)
    rs = np.sin(ang)
    rs[:, 0] = np.cos(np.pi * t)
    tau = np.arange(256) + 256
    angi = 2 * np.pi * np.outer(f, tau) / NFFT
    ric = (2.0 / NFFT) * np.cos(angi)
    ric[0, :] = 1.0 / NFFT
    ris = (2.0 / NFFT) * np.sin(angi)
    ris[0, :] = ((-1.0) ** np.arange(256)) / NFFT
    return (rc.astype(BF), rs.astype(BF), ric.astype(BF), ris.astype(BF))


def _col128(v):
    n = v.shape[0] // 128
    return np.ascontiguousarray(v.reshape(n, 128).T)


def kernel(L0_attn_x, L0_attn_w_in, L0_attn_b_in, L0_attn_w_a, L0_attn_b_a,
           L0_attn_w_b, L0_attn_b_b, L0_attn_w_out, L0_attn_b_out,
           L0_ffn_router_w, L0_ffn_router_b, L0_ffn_w_c_fc, L0_ffn_b_c_fc,
           L0_ffn_w_gate, L0_ffn_b_gate, L0_ffn_w_c_proj, L0_ffn_b_c_proj):
    f32 = np.float32
    x = np.ascontiguousarray(np.asarray(L0_attn_x, f32))
    rc, rs, ric, ris = _dft_consts()
    base = {
        "rw_col": np.ascontiguousarray(
            np.asarray(L0_ffn_router_w, f32).reshape(NCC, 128, E)
            .transpose(1, 0, 2).reshape(128, NCC * E)),
        "rb_row": np.asarray(L0_ffn_router_b, f32).reshape(1, E),
        "iota_e": np.ascontiguousarray(
            np.broadcast_to(np.arange(E, dtype=f32), (128, E))),
        "tri_bf": np.triu(np.ones((128, 128), f32), 1).astype(BF),
        "ones_bf": np.ones((128, 128), BF),
        "wfc": np.ascontiguousarray(np.asarray(L0_ffn_w_c_fc, f32)),
        "wg": np.ascontiguousarray(np.asarray(L0_ffn_w_gate, f32)),
        "wp": np.ascontiguousarray(np.asarray(L0_ffn_w_c_proj, f32)),
        "bfc_col": np.ascontiguousarray(
            np.asarray(L0_ffn_b_c_fc, f32).reshape(E, NHC, 128)
            .transpose(2, 0, 1).reshape(128, E * NHC)),
        "bg_col": np.ascontiguousarray(
            np.asarray(L0_ffn_b_gate, f32).reshape(E, NHC, 128)
            .transpose(2, 0, 1).reshape(128, E * NHC)),
        "bp_col": np.ascontiguousarray(
            np.asarray(L0_ffn_b_c_proj, f32).reshape(E, NCC, 128)
            .transpose(2, 0, 1).reshape(128, E * NCC)),
        "w_inT": np.ascontiguousarray(np.asarray(L0_attn_w_in, f32).T),
        "b_in_row": np.asarray(L0_attn_b_in, f32).reshape(1, 2 * C),
        "w_outT": np.ascontiguousarray(np.asarray(L0_attn_w_out, f32).T),
        "b_out_col": _col128(np.asarray(L0_attn_b_out, f32)),
        "wab_rev": np.ascontiguousarray(np.stack([
            np.asarray(L0_attn_w_a, f32)[:, 0, ::-1].T,
            np.asarray(L0_attn_w_b, f32)[:, 0, ::-1].T])),
        "b_a_col": _col128(np.asarray(L0_attn_b_a, f32)),
        "b_b_col": _col128(np.asarray(L0_attn_b_b, f32)),
        "rc": rc, "rs": rs, "ric": ric, "ris": ris,
    }
    nc = build_nc()
    in_maps = [dict(base, x=np.ascontiguousarray(x[b])) for b in range(8)]
    res = run_bass_kernel_spmd(nc, in_maps, core_ids=list(range(8)))
    return np.stack([np.asarray(r["out"]) for r in res.results], axis=0)


# version 06
# speedup vs baseline: 1.3544x; 1.3544x over previous
import numpy as np
import ml_dtypes
from contextlib import ExitStack

import concourse.bass as bass
import concourse.mybir as mybir
from concourse.bass_utils import run_bass_kernel_spmd
from concourse.bass_core_rust import IndirectOffsetOnAxis

F32 = mybir.dt.float32
BF16 = mybir.dt.bfloat16
I32 = mybir.dt.int32
U32 = mybir.dt.uint32
AF = mybir.ActivationFunctionType
ALU = mybir.AluOpType
AX = mybir.AxisListType.X
BF = ml_dtypes.bfloat16

T, C, H4, E = 2048, 1024, 4096, 8
CAP, CAPP = 614, 640
NT, NCC, NHC = T // 128, C // 128, H4 // 128
NFFT, NW = 512, T // 256
HGRP = 256
NHG = H4 // HGRP
DBG_PHASES = 3
BIG = float(1 << 20)
EPS = 1e-5
SGRPS = [(0, 307), (307, 307)]


class Sem:
    __slots__ = ("h", "v")

    def __init__(self, ctx, nc, name):
        self.h = ctx.enter_context(nc.semaphore(name))
        self.v = 0

    def inc(self, inst, n=1):
        inst.then_inc(self.h, n)
        self.v += n
        return self.v


def build_nc():
    nc = bass.Bass("TRN2", target_bir_lowering=False)

    d = {}
    def din(name, shape, dt=F32):
        d[name] = nc.dram_tensor(name, shape, dt, kind="ExternalInput").ap()
        return d[name]

    dx = din("x", [T, C])
    din("rw_col", [128, NCC * E])
    din("rb_row", [1, E])
    din("iota_e", [128, E])
    din("tri_bf", [128, 128])
    din("ones_bf", [128, 128])
    din("wfc", [E, NHG, 128, NCC * HGRP])
    din("wg", [E, NHG, 128, NCC * HGRP])
    din("wp", [E, NCC, 128, NHC * 128])
    din("bfc_col", [128, E * NHC])
    din("bg_col", [128, E * NHC])
    din("bp_col", [128, E * NCC])
    din("w_inT", [C, 2 * C])
    din("b_in_row", [1, 2 * C])
    din("w_outT", [2 * C, C])
    din("b_out_col", [128, NCC])
    din("wab_rev", [2, 256, C])
    din("b_a_col", [128, NCC])
    din("b_b_col", [128, NCC])
    din("rc", [NFFT, 256], BF16)
    din("rs", [NFFT, 256], BF16)
    din("ric", [256, 256], BF16)
    din("ris", [256, 256], BF16)
    dout = nc.dram_tensor("out", [T, C], F32, kind="ExternalOutput").ap()
    dxrows = nc.dram_tensor("xrows", [E * CAPP + 1, C], BF16, kind="Internal").ap()
    dorows = nc.dram_tensor("orows", [E * CAPP, C], BF16, kind="Internal").ap()

    ctx = ExitStack()
    S = lambda name, shape, dt=F32: ctx.enter_context(nc.sbuf_tensor(name, shape, dt))

    X = S("X", [128, NT * C])
    XBF32 = S("XBF32", [128, NT * C], BF16)
    MSILA = S("MSILA", [128, NHC * CAP], BF16)
    _msf = MSILA[:].bitcast(F32)
    TRI = _msf[:, 8192:8320]
    ONESC = _msf[:, 8320:8448]
    IDB = S("IDB", [128, 128], BF16)
    IDF = S("IDF", [128, 128], F32)
    ONESB = S("ONESB", [1, 128], BF16)
    ONESF = S("ONESF", [1, 128], F32)
    SCRA = S("SCRA", [128, 128], F32)
    SCRB = S("SCRB", [128, 128], F32)
    SMALL = S("SMALL", [128, 1036], F32)
    _sm = [0]

    def sm(n, dt=F32):
        a = _sm[0]
        _sm[0] += n
        assert _sm[0] <= 1036
        v = SMALL[:, a:a + n]
        return v if dt is F32 else v.bitcast(dt)

    IOTAE = sm(E)
    RWC = sm(NCC * E)
    MEANS = sm(NT)
    VARS = sm(NT)
    NEGM = sm(NT)
    RSTD = sm(NT)
    MXA = sm(E)
    ECOLS = sm(2 * NT)
    DSUB = sm(NT)
    P0T = sm(NT)
    HP0 = sm(NT)
    HP1 = sm(NT)
    OH = sm(NT * 16)
    CSTA = sm(NT * 16)
    TOTS = sm(16)
    PB = sm(256)
    RBR = S("RBR", [1, E], F32)
    IXA = S("IXA", [128, E], U32)
    SIDX = S("SIDX", [128, 2 * NT], I32)
    RIDX = S("RIDX", [128, 2 * NT], I32)
    OSBA = S("OSBA", [128, 2 * CAP], BF16)
    XNT = S("XNT", [128, C])
    XNTT = S("XNTT", [128, C])
    WFCB = [S("WFCB0", [128, NCC * HGRP], BF16), S("WFCB1", [128, NCC * HGRP], BF16)]
    WGB = [S("WGB0", [128, NCC * HGRP], BF16), S("WGB1", [128, NCC * HGRP], BF16)]
    WPB = [S("WPB0", [128, NHC * 128], BF16), S("WPB1", [128, NHC * 128], BF16)]
    OROW0 = S("OROW0", [128, C], BF16)
    MSB = [S("MSB0", [128, 320], F32), S("MSB1", [128, 320], F32)]
    OSBB = S("OSBB", [128, 3 * CAP], BF16)
    OSBC = S("OSBC", [128, 3 * CAP], BF16)
    BFCC = S("BFCC", [128, E * NHC], F32)
    BGC = S("BGC", [128, E * NHC], F32)
    BPC = S("BPC", [128, E * NCC], F32)
    DUM = S("DUM", [1, 16], F32)

    sX = Sem(ctx, nc, "sX")
    sW = Sem(ctx, nc, "sW")
    sSC = Sem(ctx, nc, "sSC")
    sER = Sem(ctx, nc, "sER")
    sOR = Sem(ctx, nc, "sOR")
    sGA = Sem(ctx, nc, "sGA")
    sPE = Sem(ctx, nc, "sPE")
    sDV = Sem(ctx, nc, "sDV")
    sAC = Sem(ctx, nc, "sAC")
    sPO = Sem(ctx, nc, "sPO")
    sST = Sem(ctx, nc, "sST")
    sBAR = Sem(ctx, nc, "sBAR")

    blk = nc.Block().__enter__()
    TEN, VEC, SCA, GPS, SYN = blk.tensor, blk.vector, blk.scalar, blk.gpsimd, blk.sync

    @SYN
    def _(sync):
        sync.dma_start(out=X[:].rearrange("p (i c) -> p i c", i=NT),
                       in_=dx.rearrange("(i p) c -> p i c", p=128)).then_inc(sX.h, 16)
        sX.v += 16
        for name, dst in [("rw_col", RWC), ("rb_row", RBR), ("iota_e", IOTAE),
                          ("tri_bf", TRI), ("ones_bf", ONESC),
                          ("bfc_col", BFCC), ("bg_col", BGC), ("bp_col", BPC)]:
            sync.dma_start(out=dst[:], in_=d[name][:]).then_inc(sX.h, 16)
            sX.v += 16
    setup_loads = sX.v

    @GPS
    def _(gpsimd):
        gpsimd.memset(IDB[:], 0.0)
        gpsimd.affine_select(out=IDB[:], in_=IDB[:], compare_op=ALU.not_equal,
                             fill=1.0, base=0, pattern=[[-1, 128]], channel_multiplier=1)
        gpsimd.memset(IDF[:], 0.0)
        gpsimd.affine_select(out=IDF[:], in_=IDF[:], compare_op=ALU.not_equal,
                             fill=1.0, base=0, pattern=[[-1, 128]], channel_multiplier=1)
        gpsimd.memset(ONESB[:], 1.0)
        gpsimd.memset(DUM[:], 0.0)
        inst = gpsimd.memset(ONESF[:], 1.0)
        sPO.inc(inst)
    setup_po = sPO.v

    def barrier():
        for dec, kind in ((TEN, "n"), (VEC, "v"), (SCA, "s"), (GPS, "v"), (SYN, "n")):
            @dec
            def _(eng, kind=kind):
                if kind == "v":
                    inst = eng.memset(DUM[0:1, 0:4], 0.0)
                elif kind == "s":
                    inst = eng.activation(out=DUM[0:1, 4:8], in_=DUM[0:1, 0:4], func=AF.Copy)
                else:
                    inst = eng.nop()
                inst.then_inc(sBAR.h, 1)
        sBAR.v += 5
        tgt = sBAR.v
        for dec in (TEN, VEC, SCA, GPS, SYN):
            @dec
            def _(eng):
                eng.wait_ge(sBAR.h, tgt)

    def emit_ln(first=False, scratch=None):
        sq_pl = {i: sAC.v + i + 1 for i in range(NT)}
        var_done = {}
        for i in range(NT):
            @VEC
            def _(vector, i=i):
                if first and i == 0:
                    vector.wait_ge(sX.h, setup_loads)
                    vector.wait_ge(sPO.h, setup_po)
                xt = X[:, i * C:(i + 1) * C]
                inst = vector.reduce_sum(out=MEANS[:, i:i + 1], in_=xt, axis=AX)
                sDV.inc(inst)
                vector.wait_ge(sDV.h, sDV.v)
                inst = vector.tensor_scalar(out=NEGM[:, i:i + 1],
                                            in0=MEANS[:, i:i + 1],
                                            scalar1=-1.0 / C, scalar2=None,
                                            op0=ALU.mult)
                sDV.inc(inst)
                vector.wait_ge(sDV.h, sDV.v)
                inst = vector.tensor_scalar(out=scratch[:], in0=xt,
                                            scalar1=NEGM[:, i:i + 1],
                                            scalar2=None, op0=ALU.add)
                sDV.inc(inst)
                vector.wait_ge(sDV.h, sDV.v)
                inst = vector.tensor_tensor(out=scratch[:], in0=scratch[:],
                                            in1=scratch[:], op=ALU.mult)
                sDV.inc(inst)
                vector.wait_ge(sDV.h, sDV.v)
                inst = vector.reduce_sum(out=VARS[:, i:i + 1], in_=scratch[:],
                                         axis=AX)
                sDV.inc(inst)
                vector.wait_ge(sDV.h, sDV.v)
                inst = vector.tensor_scalar(out=VARS[:, i:i + 1],
                                            in0=VARS[:, i:i + 1],
                                            scalar1=1.0 / C, scalar2=float(EPS),
                                            op0=ALU.mult, op1=ALU.add)
                sDV.inc(inst)
                var_done[i] = sDV.v

            @VEC
            def _(vector, i=i):
                vector.wait_ge(sAC.h, sq_pl[i])
                inst = vector.reciprocal(out=RSTD[:, i:i + 1], in_=P0T[:, i:i + 1])
                sDV.inc(inst)
                vector.wait_ge(sDV.h, sDV.v)

        @SCA
        def _(scalar):
            for i in range(NT):
                scalar.wait_ge(sDV.h, var_done[i])
                inst = scalar.activation(out=P0T[:, i:i + 1], in_=VARS[:, i:i + 1],
                                         func=AF.Sqrt)
                sAC.inc(inst)
                assert sAC.v == sq_pl[i]
        return sDV.v

    def emit_ffn(call):
        tag = f"f{call}"

        rctx = ExitStack()
        rS = lambda n, shape, dt=F32: rctx.enter_context(nc.sbuf_tensor(f"{n}_{tag}", shape, dt))
        rP = lambda n, shape, dt=F32: rctx.enter_context(nc.psum_tensor(f"{n}_{tag}", shape, dt))
        XNBF = XBF32
        pTR = [rP("pTR0", [128, 512]), rP("pTR1", [128, 512])]
        pLG = [rP("pLG0", [128, E]), rP("pLG1", [128, E])]
        pCS = [rP("pCS0", [128, 16]), rP("pCS1", [128, 16])]
        pTOT = rP("pTOT", [128, 16])

        if call == 1:
            @SYN
            def _(sync):
                for name, dst in [("tri_bf", TRI), ("ones_bf", ONESC),
                                  ("bfc_col", BFCC), ("bg_col", BGC)]:
                    inst = sync.dma_start(out=dst[:], in_=d[name][:])
                    sX.inc(inst, 16)
        reload_mark = sX.v

        emit_ln(first=(call == 0), scratch=XNT[:])

        xnf_done, xnbf_done, trev_done, lgev_done = {}, {}, {}, {}
        tr_mm, lg_mm, cs_mm = {}, {}, {}
        oh_done, csev_done = {}, {}

        for i in range(NT):
            @VEC
            def _(vector, i=i):
                if i >= 1:
                    vector.wait_ge(sPE.h, tr_mm[(i - 1, 1)])
                inst = vector.tensor_scalar(
                    out=XNT[:],
                    in0=X[:, i * C:(i + 1) * C],
                    scalar1=NEGM[:, i:i + 1], scalar2=RSTD[:, i:i + 1],
                    op0=ALU.add, op1=ALU.mult)
                sDV.inc(inst)
                xnf_done[i] = sDV.v
                vector.wait_ge(sDV.h, sDV.v)
                inst = vector.tensor_copy(out=XNBF[:, i * C:(i + 1) * C],
                                          in_=XNT[:])
                sDV.inc(inst)
                xnbf_done[i] = sDV.v

            for half in range(2):
                @TEN
                def _(tensor, i=i, half=half):
                    tensor.wait_ge(sDV.h, xnf_done[i])
                    k = i * 2 + half
                    if k >= 2:
                        tensor.wait_ge(sDV.h, trev_done[k - 2])
                    for j in range(4):
                        cc = half * 4 + j
                        inst = tensor.transpose(
                            pTR[half][:, j * 128:(j + 1) * 128],
                            XNT[:, cc * 128:(cc + 1) * 128],
                            IDF[:])
                    sPE.inc(inst)
                    tr_mm[(i, half)] = sPE.v

                @VEC
                def _(vector, i=i, half=half):
                    vector.wait_ge(sPE.h, tr_mm[(i, half)])
                    if half == 0 and i >= 1:
                        vector.wait_ge(sPE.h, lg_mm[i - 1])
                    inst = vector.tensor_copy(
                        out=XNTT[:, half * 512:(half + 1) * 512],
                        in_=pTR[half][:])
                    sDV.inc(inst)
                    trev_done[i * 2 + half] = sDV.v

            @TEN
            def _(tensor, i=i):
                if i == 0 and call == 0:
                    tensor.wait_ge(sX.h, setup_loads)
                tensor.wait_ge(sDV.h, trev_done[i * 2 + 1])
                if i >= 2:
                    tensor.wait_ge(sDV.h, lgev_done[i - 2])
                pb = pLG[i % 2]
                for cc in range(NCC):
                    tensor.matmul(pb[:],
                                  XNTT[:, cc * 128:(cc + 1) * 128],
                                  RWC[:, cc * E:(cc + 1) * E],
                                  start=(cc == 0), stop=False)
                inst = tensor.matmul(pb[:], ONESF[0:1, :], RBR[0:1, :],
                                     start=False, stop=True)
                sPE.inc(inst)
                lg_mm[i] = sPE.v

            @VEC
            def _(vector, i=i):
                vector.wait_ge(sPE.h, lg_mm[i])
                mx = MXA[:]
                inst = vector.max(out=mx, in_=pLG[i % 2][:])
                sDV.inc(inst)
                vector.wait_ge(sDV.h, sDV.v)
                inst = vector.max_index(out=IXA[:], in_max=mx, in_values=pLG[i % 2][:])
                sDV.inc(inst)
                lgev_done[i] = sDV.v
                vector.wait_ge(sDV.h, sDV.v)
                inst = vector.tensor_copy(out=ECOLS[:, 2 * i:2 * i + 2], in_=IXA[:, 0:2])
                sDV.inc(inst)
                vector.tensor_tensor(out=DSUB[:, i:i + 1], in0=mx[:, 0:1],
                                     in1=mx[:, 1:2], op=ALU.subtract)
                vector.wait_ge(sDV.h, sDV.v)
                vector.tensor_scalar(out=OH[:, i * 16:i * 16 + 8], in0=IOTAE[:],
                                     scalar1=ECOLS[:, 2 * i:2 * i + 1], scalar2=None,
                                     op0=ALU.is_equal)
                inst = vector.tensor_scalar(out=OH[:, i * 16 + 8:i * 16 + 16], in0=IOTAE[:],
                                            scalar1=ECOLS[:, 2 * i + 1:2 * i + 2],
                                            scalar2=None, op0=ALU.is_equal)
                sDV.inc(inst)
                oh_done[i] = sDV.v

        for i in range(NT):
            @TEN
            def _(tensor, i=i):
                tensor.wait_ge(sDV.h, oh_done[i])
                if i == 0 and call == 1:
                    tensor.wait_ge(sX.h, reload_mark)
                if i >= 2:
                    tensor.wait_ge(sDV.h, csev_done[i - 2])
                pb = pCS[i % 2]
                for j in range(i):
                    tensor.matmul(pb[:], ONESC[:], OH[:, j * 16:(j + 1) * 16],
                                  start=(j == 0), stop=False)
                inst = tensor.matmul(pb[:], TRI[:], OH[:, i * 16:(i + 1) * 16],
                                     start=(i == 0), stop=True)
                sPE.inc(inst)
                cs_mm[i] = sPE.v

            @VEC
            def _(vector, i=i):
                vector.wait_ge(sPE.h, cs_mm[i])
                inst = vector.tensor_copy(out=CSTA[:, i * 16:(i + 1) * 16],
                                          in_=pCS[i % 2][:])
                sDV.inc(inst)
                csev_done[i] = sDV.v

        @TEN
        def _(tensor):
            tensor.wait_ge(sDV.h, oh_done[NT - 1])
            for j in range(NT):
                inst = tensor.matmul(pTOT[:], ONESC[:], OH[:, j * 16:(j + 1) * 16],
                                     start=(j == 0), stop=(j == NT - 1))
            sPE.inc(inst)
            cs_mm["tots"] = sPE.v

        @VEC
        def _(vector):
            vector.wait_ge(sPE.h, cs_mm["tots"])
            inst = vector.tensor_copy(out=TOTS[:], in_=pTOT[:])
            sDV.inc(inst)
        tots_ev = sDV.v

        P_POS = PB[:, 0:32]
        P_SCL = PB[:, 32:64]
        P_DM = PB[:, 64:96]
        P_PC = PB[:, 96:128]
        P_T1 = PB[:, 128:160]
        P_T2 = PB[:, 160:192]

        @VEC
        def _(vector):
            vector.wait_ge(sDV.h, tots_ev)
            for i in range(NT):
                inst = vector.tensor_tensor(
                    out=CSTA[:, i * 16 + 8:i * 16 + 16],
                    in0=CSTA[:, i * 16 + 8:i * 16 + 16],
                    in1=TOTS[:, 0:8], op=ALU.add)
                sDV.inc(inst)
            vector.wait_ge(sDV.h, sDV.v)
            inst = vector.tensor_tensor(out=CSTA[:], in0=OH[:], in1=CSTA[:],
                                        op=ALU.mult)
            sDV.inc(inst)
            inst = vector.tensor_scalar(out=P_SCL, in0=ECOLS[:],
                                        scalar1=float(CAPP), scalar2=None,
                                        op0=ALU.mult)
            sDV.inc(inst)
            vector.wait_ge(sDV.h, sDV.v)
            inst = vector.reduce_sum(
                out=P_POS, in_=CSTA[:].rearrange("p (t e) -> p t e", t=32), axis=AX)
            sDV.inc(inst)
            vector.wait_ge(sDV.h, sDV.v)
            inst = vector.tensor_scalar(out=P_DM, in0=P_POS, scalar1=613.5,
                                        scalar2=BIG, op0=ALU.is_gt, op1=ALU.mult)
            sDV.inc(inst)
            inst = vector.tensor_scalar(out=P_PC, in0=P_POS, scalar1=613.0,
                                        scalar2=None, op0=ALU.min)
            sDV.inc(inst)
            inst = vector.tensor_tensor(out=P_T1, in0=P_SCL, in1=P_POS, op=ALU.add)
            sDV.inc(inst)
            vector.wait_ge(sDV.h, sDV.v)
            inst = vector.tensor_tensor(out=P_T2, in0=P_T1, in1=P_DM, op=ALU.add)
            sDV.inc(inst)
            vector.wait_ge(sDV.h, sDV.v)
            inst = vector.tensor_scalar(out=P_T2, in0=P_T2,
                                        scalar1=float(E * CAPP), scalar2=None,
                                        op0=ALU.min)
            sDV.inc(inst)
            inst = vector.tensor_tensor(out=P_T1, in0=P_SCL, in1=P_PC, op=ALU.add)
            sDV.inc(inst)
            vector.wait_ge(sDV.h, sDV.v)
            inst = vector.tensor_copy(out=SIDX[:], in_=P_T2)
            sDV.inc(inst)
            inst = vector.tensor_copy(out=RIDX[:], in_=P_T1)
            sDV.inc(inst)
            vector.wait_ge(sDV.h, sDV.v)
        all_idx = sDV.v

        for i in range(NT):
            @GPS
            def _(gpsimd, i=i):
                if i == 0:
                    gpsimd.wait_ge(sDV.h, all_idx)
                for k in range(2):
                    inst = gpsimd.indirect_dma_start(
                        out=dxrows,
                        out_offset=IndirectOffsetOnAxis(SIDX[:, 2 * i + k:2 * i + k + 1], 0),
                        in_=XNBF[:, i * C:(i + 1) * C], in_offset=None)
                    sSC.inc(inst, 16)
        scatter_done = sSC.v

        @SCA
        def _(scalar):
            scalar.wait_ge(sDV.h, all_idx)
            inst = scalar.activation(out=P0T[:], in_=DSUB[:], func=AF.Sigmoid)
            sAC.inc(inst)
        sig_done = sAC.v

        @VEC
        def _(vector):
            vector.wait_ge(sAC.h, sig_done)
            vector.tensor_scalar(out=HP0[:], in0=P0T[:], scalar1=0.5, scalar2=None,
                                 op0=ALU.mult)
            inst = vector.tensor_scalar(out=HP1[:], in0=P0T[:], scalar1=-0.5,
                                        scalar2=0.5, op0=ALU.mult, op1=ALU.add)
            sDV.inc(inst)
        hp_done = sDV.v

        rctx.close()

        pctx = ExitStack()
        pS = lambda n, shape, dt=F32: pctx.enter_context(nc.sbuf_tensor(f"{n}_{tag}", shape, dt))
        pP = lambda n, shape, dt=F32: pctx.enter_context(nc.psum_tensor(f"{n}_{tag}", shape, dt))
        XSTG = XBF32[:, 0:5 * C]
        EINPT = [XBF32[:, 5 * C:10 * C], XBF32[:, 10 * C:15 * C]]
        MSIL = MSILA
        def osb(cc, a, b):
            if cc < 2:
                return OSBA[:, cc * CAP + a:cc * CAP + b]
            if cc < 5:
                return OSBB[:, (cc - 2) * CAP + a:(cc - 2) * CAP + b]
            return OSBC[:, (cc - 5) * CAP + a:(cc - 5) * CAP + b]
        OROW = [OROW0, XBF32[:, 15 * C:16 * C]]
        pH = [pP("pH0", [128, 320]), pP("pH1", [128, 320])]
        pG = [pP("pG0", [128, 320]), pP("pG1", [128, 320])]
        pPj = [pP("pPj0", [128, 320]), pP("pPj1", [128, 320])]
        pOT = pP("pOT", [128, 1024], BF16)
        pXP = pP("pXP", [128, 1024], BF16)

        wfc_mark, wp_mark, wfc_used, wp_used = {}, {}, {}, {}
        er_mark, einp_used, xstg_free, gmark = {}, {}, {}, {}
        xp_mm, xp_ev = {}, {}
        hg_mm, h_ev, g_mu, sil_mk = {}, {}, {}, {}
        pj_mm, o_ev, otr_mm, orow_ev = {}, {}, {}, {}
        silu_last, proj_last, otr_last = {}, {}, {}
        orow_st_list = []

        for e in range(E):
            @SYN
            def _(sync, e=e):
                if e == 0:
                    sync.wait_ge(sSC.h, scatter_done)
                if e >= 1:
                    sync.wait_ge(sPE.h, xstg_free[e - 1])
                inst = sync.dma_start(
                    out=XSTG.rearrange("p (t c) -> p t c", t=5),
                    in_=dxrows[e * CAPP:e * CAPP + 5 * 128, :]
                    .rearrange("(t p) c -> p t c", p=128))
                sER.inc(inst, 16)
                er_mark[e] = sER.v

            for t in range(5):
                @TEN
                def _(tensor, e=e, t=t):
                    if t == 0:
                        tensor.wait_ge(sER.h, er_mark[e])
                        if e >= 1:
                            tensor.wait_ge(sDV.h, xp_ev[(e - 1, 4)])
                    else:
                        tensor.wait_ge(sDV.h, xp_ev[(e, t - 1)])
                    for cc in range(NCC):
                        inst = tensor.transpose(
                            pXP[:, cc * 128:(cc + 1) * 128],
                            XSTG[:, t * C + cc * 128:t * C + (cc + 1) * 128],
                            IDB[:])
                    sPE.inc(inst)
                    xp_mm[(e, t)] = sPE.v
                    if t == 4:
                        xstg_free[e] = sPE.v

                @VEC
                def _(vector, e=e, t=t):
                    vector.wait_ge(sPE.h, xp_mm[(e, t)])
                    if e >= 2 and t == 0:
                        vector.wait_ge(sPE.h, einp_used[e - 2])
                    inst = vector.tensor_copy(
                        out=EINPT[e % 2]
                        .rearrange("p (cc s) -> p cc s", cc=NCC)[:, :, t * 128:(t + 1) * 128],
                        in_=pXP[:].rearrange("p (cc j) -> p cc j", cc=NCC))
                    sDV.inc(inst)
                    xp_ev[(e, t)] = sDV.v

            for hg in range(NHG):
                _npend = (hg % 2) * 2 * (HGRP // 128)
                for j in range(2 * (HGRP // 128)):
                    sil_mk[(e, hg * 2 * (HGRP // 128) + j)] = sAC.v + _npend + j + 1

                @GPS
                def _(gpsimd, e=e, hg=hg):
                    gidx = e * NHG + hg
                    if gidx >= 2:
                        gpsimd.wait_ge(sPE.h, wfc_used[gidx - 2])
                    buf = gidx % 2
                    inst = gpsimd.dma_start(out=WFCB[buf][:], in_=d["wfc"][e, hg])
                    sW.inc(inst, 16)
                    inst = gpsimd.dma_start(out=WGB[buf][:], in_=d["wg"][e, hg])
                    sW.inc(inst, 16)
                    wfc_mark[e * NHG + hg] = sW.v

                for hsub in range(HGRP // 128):
                    hc = hg * (HGRP // 128) + hsub
                    for gi, (s0, sn) in enumerate(SGRPS):
                        q = hc * 2 + gi

                        @TEN
                        def _(tensor, e=e, hg=hg, hc=hc, gi=gi, s0=s0, sn=sn, q=q):
                            ebuf = EINPT[e % 2]
                            buf = (e * NHG + hg) % 2
                            pi = q % 2
                            if q == 0:
                                tensor.wait_ge(sDV.h, xp_ev[(e, 4)])
                            if gi == 0 and hc % (HGRP // 128) == 0:
                                tensor.wait_ge(sW.h, wfc_mark[e * NHG + hg])
                            if q >= 2:
                                wk = (e, q - 2)
                            elif e > 0:
                                wk = (e - 1, 2 * NHC - 2 + q)
                            else:
                                wk = None
                            if wk is not None:
                                tensor.wait_ge(sDV.h, g_mu[wk])
                            hof = hc % (HGRP // 128)
                            for cc in range(NCC):
                                tensor.matmul(
                                    pH[pi][:, 0:sn],
                                    WFCB[buf][:, cc * HGRP + hof * 128:cc * HGRP + (hof + 1) * 128],
                                    ebuf[:, cc * CAPP + s0:cc * CAPP + s0 + sn],
                                    start=(cc == 0), stop=(cc == NCC - 1))
                            for cc in range(NCC):
                                inst = tensor.matmul(
                                    pG[pi][:, 0:sn],
                                    WGB[buf][:, cc * HGRP + hof * 128:cc * HGRP + (hof + 1) * 128],
                                    ebuf[:, cc * CAPP + s0:cc * CAPP + s0 + sn],
                                    start=(cc == 0), stop=(cc == NCC - 1))
                            sPE.inc(inst)
                            hg_mm[(e, q)] = sPE.v
                            if gi == 1 and hof == (HGRP // 128) - 1:
                                wfc_used[e * NHG + hg] = sPE.v
                            if q == 2 * NHC - 1:
                                einp_used[e] = sPE.v

                        @VEC
                        def _(vector, e=e, hc=hc, s0=s0, sn=sn, q=q):
                            pi = q % 2
                            vector.wait_ge(sPE.h, hg_mm[(e, q)])
                            if call == 1 and e == 0 and q == 0:
                                vector.wait_ge(sX.h, reload_mark)
                            if q >= 2:
                                vector.wait_ge(sAC.h, sil_mk[(e, q - 2)])
                            elif e > 0:
                                vector.wait_ge(sAC.h, sil_mk[(e - 1, 2 * NHC - 2 + q)])
                            inst = vector.tensor_scalar(
                                out=MSB[pi][:, 0:sn], in0=pH[pi][:, 0:sn],
                                scalar1=BFCC[:, e * NHC + hc:e * NHC + hc + 1],
                                scalar2=None, op0=ALU.add)
                            sDV.inc(inst)
                            h_ev[(e, q)] = sDV.v
                            vector.wait_ge(sDV.h, sDV.v)
                            inst = vector.scalar_tensor_tensor(
                                out=MSB[pi][:, 0:sn], in0=pG[pi][:, 0:sn],
                                scalar=BGC[:, e * NHC + hc:e * NHC + hc + 1],
                                in1=MSB[pi][:, 0:sn], op0=ALU.add, op1=ALU.mult)
                            sDV.inc(inst)
                            g_mu[(e, q)] = sDV.v

                if hg % 2 == 1:
                    @SCA
                    def _(scalar, e=e, hg=hg):
                        for hgx in (hg - 1, hg):
                            for hsub in range(HGRP // 128):
                                hc = hgx * (HGRP // 128) + hsub
                                for gi, (s0, sn) in enumerate(SGRPS):
                                    q = hc * 2 + gi
                                    scalar.wait_ge(sDV.h, g_mu[(e, q)])
                                    if e > 0 and q == 0:
                                        scalar.wait_ge(sPE.h, proj_last[e - 1])
                                    inst = scalar.activation(
                                        out=MSIL[:, hc * CAP + s0:hc * CAP + s0 + sn],
                                        in_=MSB[q % 2][:, 0:sn], func=AF.Silu)
                                    sAC.inc(inst)
                                    assert sil_mk[(e, q)] == sAC.v
                                    if q == 2 * NHC - 1:
                                        silu_last[e] = sAC.v

            for cc in range(NCC):
                @GPS
                def _(gpsimd, e=e, cc=cc):
                    cidx = e * NCC + cc
                    if cidx >= 2:
                        gpsimd.wait_ge(sPE.h, wp_used[cidx - 2])
                    inst = gpsimd.dma_start(out=WPB[cidx % 2][:], in_=d["wp"][e, cc])
                    sW.inc(inst, 16)
                    wp_mark[cidx] = sW.v

                for gi, (s0, sn) in enumerate(SGRPS):
                    q = cc * 2 + gi

                    @TEN
                    def _(tensor, e=e, cc=cc, gi=gi, s0=s0, sn=sn, q=q):
                        if q == 0:
                            tensor.wait_ge(sAC.h, silu_last[e])
                        if gi == 0:
                            tensor.wait_ge(sW.h, wp_mark[e * NCC + cc])
                        if q >= 2:
                            tensor.wait_ge(sDV.h, o_ev[(e, q - 2)])
                        elif e > 0:
                            tensor.wait_ge(sDV.h, o_ev[(e - 1, 2 * NCC - 2 + q)])
                        wb = WPB[(e * NCC + cc) % 2]
                        pi = q % 2
                        for hcj in range(NHC):
                            inst = tensor.matmul(
                                pPj[pi][:, 0:sn], wb[:, hcj * 128:(hcj + 1) * 128],
                                MSIL[:, hcj * CAP + s0:hcj * CAP + s0 + sn],
                                start=(hcj == 0), stop=(hcj == NHC - 1))
                        sPE.inc(inst)
                        pj_mm[(e, q)] = sPE.v
                        if gi == 1:
                            wp_used[e * NCC + cc] = sPE.v
                        if q == 2 * NCC - 1:
                            proj_last[e] = sPE.v

                    @VEC
                    def _(vector, e=e, cc=cc, s0=s0, sn=sn, q=q):
                        vector.wait_ge(sPE.h, pj_mm[(e, q)])
                        if e > 0 and q == 0:
                            vector.wait_ge(sPE.h, otr_last[e - 1])
                        pi = q % 2
                        inst = vector.tensor_scalar(
                            out=osb(cc, s0, s0 + sn),
                            in0=pPj[pi][:, 0:sn],
                            scalar1=BPC[:, e * NCC + cc:e * NCC + cc + 1],
                            scalar2=None, op0=ALU.add)
                        sDV.inc(inst)
                        o_ev[(e, q)] = sDV.v

            for sc_i in range(5):
                s0 = sc_i * 128
                sn = min(128, CAP - s0)

                @TEN
                def _(tensor, e=e, sc_i=sc_i, s0=s0, sn=sn):
                    tensor.wait_ge(sDV.h, o_ev[(e, 2 * NCC - 1)])
                    if sc_i >= 1:
                        tensor.wait_ge(sDV.h, orow_ev[(e, sc_i - 1)])
                    elif e > 0:
                        tensor.wait_ge(sDV.h, orow_ev[(e - 1, 4)])
                    for cc in range(NCC):
                        inst = tensor.transpose(
                            pOT[0:sn, cc * 128:(cc + 1) * 128],
                            osb(cc, s0, s0 + sn), IDB[:])
                    sPE.inc(inst)
                    otr_mm[(e, sc_i)] = sPE.v
                    if sc_i == 4:
                        otr_last[e] = sPE.v

                @VEC
                def _(vector, e=e, sc_i=sc_i, sn=sn):
                    vector.wait_ge(sPE.h, otr_mm[(e, sc_i)])
                    if len(orow_st_list) >= 2:
                        vector.wait_ge(sOR.h, orow_st_list[-2])
                    inst = vector.tensor_copy(out=OROW[sc_i % 2][0:sn, :], in_=pOT[0:sn, :])
                    sDV.inc(inst)
                    orow_ev[(e, sc_i)] = sDV.v

                @SCA
                def _(scalar, e=e, sc_i=sc_i, s0=s0, sn=sn):
                    scalar.wait_ge(sDV.h, orow_ev[(e, sc_i)])
                    inst = scalar.dma_start(
                        out=dorows[e * CAPP + s0:e * CAPP + s0 + sn, :],
                        in_=OROW[sc_i % 2][0:sn, :])
                    sOR.inc(inst, 16)
                orow_st_list.append(sOR.v)
        or_all = sOR.v

        comb = {}
        for i in range(NT):
            @GPS
            def _(gpsimd, i=i):
                if i == 0:
                    gpsimd.wait_ge(sOR.h, or_all)
                if i >= 1:
                    gpsimd.wait_ge(sDV.h, comb[i - 1])
                for k in range(2):
                    inst = gpsimd.indirect_dma_start(
                        out=OROW[k][:], out_offset=None,
                        in_=dorows,
                        in_offset=IndirectOffsetOnAxis(RIDX[:, 2 * i + k:2 * i + k + 1], 0))
                    sGA.inc(inst, 16)
            gmark[i] = sGA.v

            @VEC
            def _(vector, i=i):
                if i == 0:
                    vector.wait_ge(sDV.h, hp_done)
                vector.wait_ge(sGA.h, gmark[i])
                xt = X[:, i * C:(i + 1) * C]
                inst = vector.scalar_tensor_tensor(
                    out=xt, in0=OROW[0][:], scalar=HP0[:, i:i + 1], in1=xt,
                    op0=ALU.mult, op1=ALU.add)
                sDV.inc(inst)
                vector.wait_ge(sDV.h, sDV.v)
                inst = vector.scalar_tensor_tensor(
                    out=xt, in0=OROW[1][:], scalar=HP1[:, i:i + 1], in1=xt,
                    op0=ALU.mult, op1=ALU.add)
                sDV.inc(inst)
                comb[i] = sDV.v
        pctx.close()
        return comb[NT - 1]

    def emit_attn():
        actx = ExitStack()
        aS = lambda n, shape, dt=F32: actx.enter_context(nc.sbuf_tensor(n, shape, dt))
        aP = lambda n, shape, dt=F32: actx.enter_context(nc.psum_tensor(n, shape, dt))

        WINT = XBF32
        WOUTT = MSILA[:, 0:16 * C]
        SPECP = [WFCB[0][:], WFCB[1][:], WGB[0][:], WGB[1][:]]
        ZBUFP = [WPB[0][:, 0:2048], WPB[0][:, 2048:4096],
                 WPB[1][:, 0:2048], WPB[1][:, 2048:4096]]
        GATP0 = aS("GAT0", [128, 2048], BF16)
        GATP1 = aS("GAT1", [128, 1024], BF16)

        def SPEC(c0, n):
            return SPECP[c0 // 2048][:, c0 % 2048:c0 % 2048 + n]
        YU = aS("YU", [128, 2 * 1024], BF16)
        YVH = [BFCC[:].bitcast(BF16)[:, 0:512], BGC[:].bitcast(BF16)[:, 0:512],
               XNT[:].bitcast(BF16)[:, 1024:1536],
               XNTT[:].bitcast(BF16)[:, 1536:2048]]

        CAB = XNT[:].bitcast(BF16)[:, 0:2 * 512]
        XN2T = XNTT[:].bitcast(BF16)[:, 0:C]
        RCB = OSBA[:, 0:1024]
        RSB = OSBB[:, 0:1024]
        RICB = OSBB[:, 1024:1536]
        RISB = OSBC[:, 0:512]
        XNBF2 = OSBC[:, 512:1536]
        ZS = MSILA[:, 16 * C:16 * C + 2048]
        TMW = MSILA[:, 16 * C + 2048:16 * C + 2560]
        TMG = MSILA[:, 16 * C + 2560:16 * C + 3072]
        ATSB = PB[:, 0:256]
        SGV = [MSB[0][:].bitcast(BF16)[:, 0:512],
               MSB[1][:].bitcast(BF16)[:, 0:512],
               OROW0[:, 0:512], OROW0[:, 512:1024]]
        GBH = [GATP1[:, k * 256:(k + 1) * 256] for k in range(4)] + [
            OSBB[:, 1536:1792], OSBC[:, 1536:1792],
            SCRA[:].bitcast(BF16)[:, 0:256], SCRB[:].bitcast(BF16)[:, 0:256]]

        def GATED(c0, n):
            assert n == 256
            ec = c0 // 256
            if ec < 8:
                return GATP0[:, ec * 256:(ec + 1) * 256]
            return GBH[ec - 8]

        BINRV = [OH[:].bitcast(BF16)[0:1, 0:512],
                 CSTA[:].bitcast(BF16)[0:1, 0:512],
                 XNT[:].bitcast(BF16)[0:1, 1536:2048],
                 XNTT[:].bitcast(BF16)[0:1, 1024:1536]]
        BACOL = TOTS[:, 0:8]
        BBCOL = TOTS[:, 8:16]
        BOCOL = MXA[:, 0:8]

        pZ = aP("pZ", [128, 512])
        pXT = aP("pXT", [128, 1024], BF16)
        pFW = [aP("pFW0", [128, 512]), aP("pFW1", [128, 512])]
        pIV = [aP("pIV0", [128, 256]), aP("pIV1", [128, 256])]
        pOP = aP("pOP", [128, 256])
        pAT = aP("pAT", [128, 256])

        @SYN
        def _(sync):
            for dst, name in [(RICB, "ric"), (RISB, "ris")]:
                inst = sync.dma_start(
                    out=dst[:].rearrange("p (f t) -> p f t", f=2),
                    in_=d[name].rearrange("(f p) t -> p f t", p=128))
                sX.inc(inst, 16)
            for dst, name in [(BACOL, "b_a_col"), (BBCOL, "b_b_col"), (BOCOL, "b_out_col")]:
                inst = sync.dma_start(out=dst[:], in_=d[name][:])
                sX.inc(inst, 16)
            inst = sync.dma_start(
                out=RCB[:].rearrange("p (tc f) -> p tc f", tc=4),
                in_=d["rc"].rearrange("(tc p) f -> p tc f", p=128))
            sX.inc(inst, 16)
            inst = sync.dma_start(
                out=RSB[:].rearrange("p (tc f) -> p tc f", tc=4),
                in_=d["rs"].rearrange("(tc p) f -> p tc f", p=128))
            sX.inc(inst, 16)
        aconst = sX.v

        WREV = GATP0[:]

        @GPS
        def _(gpsimd):
            inst = gpsimd.dma_start(
                out=WINT[:].rearrange("p (hc c) -> p hc c", hc=NCC),
                in_=d["w_inT"].rearrange("(hc p) c -> p hc c", p=128))
            sW.inc(inst, 16)
            inst = gpsimd.dma_start(
                out=WOUTT.rearrange("p (ec h) -> p ec h", ec=16),
                in_=d["w_outT"].rearrange("(ec p) h -> p ec h", p=128))
            sW.inc(inst, 16)
            for cg in range(4):
                inst = gpsimd.dma_start(out=BINRV[cg],
                                        in_=d["b_in_row"][:, cg * 512:(cg + 1) * 512])
                sW.inc(inst, 16)
            gpsimd.memset(ZBUFP[0][:], 0.0)
            inst = gpsimd.memset(ZBUFP[1][:], 0.0)
            sPO.inc(inst)
        aw_done = sW.v
        zhist = sPO.v

        spec_mm, spec_ev, wrev_mark = {}, {}, {}
        fwd_mm, fwd_ev, yu_mk, yv_mk = {}, {}, {}, {}
        iv_mm, ca_ev, sg_mk, gat_mk = {}, {}, {}, {}
        op_mm, op_ev, at_mm, xadd = {}, {}, {}, {}
        xnb2_done, xtr_mm, xtr_ev, z_mm, z_ev = {}, {}, {}, {}, {}
        fwd_done_w = {}
        roll = {"fwbank": [None, None], "yfree": None, "slice": {}}

        sbank = [0]
        for filt in range(2):
            @GPS
            def _(gpsimd, filt=filt):
                if filt == 1:
                    gpsimd.wait_ge(sPE.h, spec_mm[(0, "last")])
                inst = gpsimd.dma_start(
                    out=WREV.rearrange("p (kc c) -> p kc c", kc=2),
                    in_=d["wab_rev"][filt].rearrange("(kc p) c -> p kc c", p=128))
                sW.inc(inst, 16)
                wrev_mark[filt] = sW.v

            for uv in range(2):
                for fc in range(2):
                    for cg in range(4):
                        bk = sbank[0] % 2
                        sbank[0] += 1

                        @TEN
                        def _(tensor, filt=filt, uv=uv, fc=fc, cg=cg, bk=bk):
                            if uv == 0 and fc == 0 and cg == 0:
                                tensor.wait_ge(sW.h, wrev_mark[filt])
                                if filt == 0:
                                    tensor.wait_ge(sX.h, aconst)
                            km = roll["fwbank"][bk]
                            if km is not None:
                                tensor.wait_ge(sDV.h, km)
                            RB = RCB if uv == 0 else RSB
                            for kc in range(2):
                                inst = tensor.matmul(
                                    pFW[bk][:, 0:256],
                                    RB[:, kc * 256 + fc * 128:kc * 256 + (fc + 1) * 128],
                                    WREV[:, kc * C + cg * 256:kc * C + (cg + 1) * 256],
                                    start=(kc == 0), stop=(kc == 1))
                            sPE.inc(inst)
                            spec_mm[(filt, uv, fc, cg)] = sPE.v
                            spec_mm[(filt, "last")] = sPE.v

                        @VEC
                        def _(vector, filt=filt, uv=uv, fc=fc, cg=cg, bk=bk):
                            vector.wait_ge(sPE.h, spec_mm[(filt, uv, fc, cg)])
                            col = (uv * 2 + fc) * 2 * C + filt * C + cg * 256
                            inst = vector.tensor_copy(out=SPEC(col, 256),
                                                      in_=pFW[bk][:, 0:256])
                            sDV.inc(inst)
                            spec_ev[(filt, uv, fc, cg)] = sDV.v
                            roll["fwbank"][bk] = sDV.v
        spec_all = sDV.v

        emit_ln(first=False, scratch=XNBF2)

        def emit_ztile(i):
            @VEC
            def _(vector, i=i):
                if i >= 1:
                    vector.wait_ge(sPE.h, xtr_mm[i - 1])
                inst = vector.tensor_scalar(
                    out=XNBF2[:],
                    in0=X[:, i * C:(i + 1) * C],
                    scalar1=NEGM[:, i:i + 1], scalar2=RSTD[:, i:i + 1],
                    op0=ALU.add, op1=ALU.mult)
                sDV.inc(inst)
                xnb2_done[i] = sDV.v

            @TEN
            def _(tensor, i=i):
                tensor.wait_ge(sDV.h, xnb2_done[i])
                if i >= 1:
                    tensor.wait_ge(sDV.h, xtr_ev[i - 1])
                for j in range(NCC):
                    inst = tensor.transpose(
                        pXT[:, j * 128:(j + 1) * 128],
                        XNBF2[:, j * 128:(j + 1) * 128],
                        IDB[:])
                sPE.inc(inst)
                xtr_mm[i] = sPE.v

            @VEC
            def _(vector, i=i):
                vector.wait_ge(sPE.h, xtr_mm[i])
                if i >= 1:
                    vector.wait_ge(sPE.h, z_mm[(i - 1, 3)])
                inst = vector.tensor_copy(out=XN2T[:], in_=pXT[:])
                sDV.inc(inst)
                xtr_ev[i] = sDV.v

            for cg in range(4):
                @TEN
                def _(tensor, i=i, cg=cg):
                    if i == 0 and cg == 0:
                        tensor.wait_ge(sW.h, aw_done)
                    tensor.wait_ge(sDV.h, xtr_ev[i])
                    if (i, cg) != (0, 0):
                        pi, pc = (i, cg - 1) if cg else (i - 1, 3)
                        tensor.wait_ge(sDV.h, z_ev[(pi, pc)])
                    for hc in range(NCC):
                        tensor.matmul(pZ[:], XN2T[:, hc * 128:(hc + 1) * 128],
                                      WINT[:, hc * 2 * C + cg * 512:hc * 2 * C + (cg + 1) * 512],
                                      start=(hc == 0), stop=False)
                    inst = tensor.matmul(pZ[:], ONESB[0:1, :], BINRV[cg],
                                         start=False, stop=True)
                    sPE.inc(inst)
                    z_mm[(i, cg)] = sPE.v

                @VEC
                def _(vector, i=i, cg=cg):
                    vector.wait_ge(sPE.h, z_mm[(i, cg)])
                    if i == 0 and cg == 0:
                        vector.wait_ge(sPO.h, zhist)
                    if i >= 4 and cg == 0:
                        vector.wait_ge(sPE.h, fwd_done_w[(i - 2) // 2])
                    slot = (i + 2) % 4
                    inst = vector.tensor_copy(
                        out=ZBUFP[slot][:, cg * 512:(cg + 1) * 512],
                        in_=pZ[:])
                    sDV.inc(inst)
                    z_ev[(i, cg)] = sDV.v

        def emit_window(w):
            fbank = [0]
            for half in range(2):
                for fc in range(2):
                    for cg in range(2):
                        ch0 = [0, 1024][cg] + half * 512
                        for uv in range(2):
                            bk = fbank[0] % 2
                            fbank[0] += 1

                            @TEN
                            def _(tensor, w=w, half=half, fc=fc, cg=cg, uv=uv, ch0=ch0, bk=bk):
                                if half == 0 and fc == 0 and cg == 0 and uv == 0:
                                    tensor.wait_ge(sDV.h, z_ev[(2 * w + 1, 3)])
                                    if w == 0:
                                        tensor.wait_ge(sDV.h, spec_all)
                                    ym = roll["yfree"]
                                    if ym is not None:
                                        tensor.wait_ge(sPE.h, ym)
                                km = roll["fwbank"][bk]
                                if km is not None:
                                    tensor.wait_ge(sDV.h, km)
                                RB = RCB if uv == 0 else RSB
                                for tj in range(4):
                                    j = 2 * w - 2 + tj
                                    slot = (j + 2) % 4
                                    inst = tensor.matmul(
                                        pFW[bk][:],
                                        RB[:, tj * 256 + fc * 128:tj * 256 + (fc + 1) * 128],
                                        ZBUFP[slot][:, ch0:ch0 + 512],
                                        start=(tj == 0), stop=(tj == 3))
                                sPE.inc(inst)
                                fwd_mm[(w, half, fc, cg, uv)] = sPE.v

                            @VEC
                            def _(vector, w=w, half=half, fc=fc, cg=cg, uv=uv, bk=bk):
                                vector.wait_ge(sPE.h, fwd_mm[(w, half, fc, cg, uv)])
                                sl = roll["slice"].get((uv, cg))
                                if sl is not None:
                                    vector.wait_ge(sPO.h, sl)
                                inst = vector.tensor_copy(
                                    out=ZS[:, (uv * 2 + cg) * 512:(uv * 2 + cg) * 512 + 512],
                                    in_=pFW[bk][:])
                                sDV.inc(inst)
                                vector.wait_ge(sDV.h, sDV.v)
                                fwd_ev[(w, half, fc, cg, uv)] = sDV.v
                                roll["fwbank"][bk] = sDV.v

                        zu = ZS[:, (0 * 2 + cg) * 512:(0 * 2 + cg) * 512 + 512]
                        zv = ZS[:, (1 * 2 + cg) * 512:(1 * 2 + cg) * 512 + 512]
                        wu = SPEC((0 * 2 + fc) * 2 * C + ch0, 512)
                        wv = SPEC((1 * 2 + fc) * 2 * C + ch0, 512)
                        yuc = YU[:, fc * 1024 + cg * 512:fc * 1024 + (cg + 1) * 512]
                        yvc = YVH[fc * 2 + cg]

                        @VEC
                        def _(vector, w=w, half=half, fc=fc, cg=cg,
                              zu=zu, zv=zv, wu=wu, wv=wv, yuc=yuc):
                            inst = vector.tensor_tensor(out=yuc, in0=zu, in1=wu, op=ALU.mult)
                            sDV.inc(inst)
                            inst = vector.tensor_tensor(out=TMW[:], in0=zv, in1=wv, op=ALU.mult)
                            sDV.inc(inst)
                            vector.wait_ge(sDV.h, sDV.v)
                            inst = vector.tensor_tensor(out=yuc, in0=yuc,
                                                        in1=TMW[:], op=ALU.subtract)
                            sDV.inc(inst)
                            if fc == 0:
                                vector.wait_ge(sDV.h, sDV.v)
                                inst = vector.tensor_tensor(out=yuc[0:1, :], in0=zu[0:1, :],
                                                            in1=wu[0:1, :], op=ALU.mult)
                                sDV.inc(inst)
                            yu_mk[(w, half, fc, cg)] = sDV.v

                        @GPS
                        def _(gpsimd, w=w, half=half, fc=fc, cg=cg,
                              zu=zu, zv=zv, wu=wu, wv=wv, yvc=yvc):
                            gpsimd.wait_ge(sDV.h, fwd_ev[(w, half, fc, cg, 1)])
                            inst = gpsimd.tensor_tensor(out=yvc, in0=zu, in1=wv, op=ALU.mult)
                            sPO.inc(inst)
                            inst = gpsimd.tensor_tensor(out=TMG[:], in0=zv, in1=wu, op=ALU.mult)
                            sPO.inc(inst)
                            gpsimd.wait_ge(sPO.h, sPO.v)
                            inst = gpsimd.tensor_tensor(out=yvc, in0=yvc, in1=TMG[:], op=ALU.add)
                            sPO.inc(inst)
                            if fc == 0:
                                gpsimd.wait_ge(sPO.h, sPO.v)
                                inst = gpsimd.tensor_tensor(out=yvc[0:1, :], in0=zv[0:1, :],
                                                            in1=wv[0:1, :], op=ALU.mult)
                                sPO.inc(inst)
                            yv_mk[(w, half, fc, cg)] = sPO.v
                            roll["slice"][(0, cg)] = sPO.v
                            roll["slice"][(1, cg)] = sPO.v

                sg_plan = {cil: sAC.v + cil + 1 for cil in range(4)}
                prev_gat = gat_mk.get(("last",))
                for cil in range(4):
                    ci = half * 4 + cil
                    for ab in range(2):
                        loc = cil if ab == 0 else 4 + cil

                        @TEN
                        def _(tensor, w=w, half=half, ci=ci, ab=ab, loc=loc, cil=cil):
                            if cil == 0 and ab == 0:
                                for fc in range(2):
                                    for cg in range(2):
                                        tensor.wait_ge(sDV.h, yu_mk[(w, half, fc, cg)])
                                        tensor.wait_ge(sPO.h, yv_mk[(w, half, fc, cg)])
                            km = ca_ev.get(("bank", ab))
                            if km:
                                tensor.wait_ge(sDV.h, km)
                            pb = pIV[ab]
                            for fc in range(2):
                                tensor.matmul(pb[:],
                                              YU[:, fc * 1024 + loc * 128:fc * 1024 + (loc + 1) * 128],
                                              RICB[:, fc * 256:(fc + 1) * 256],
                                              start=(fc == 0), stop=False)
                            for fc in range(2):
                                inst = tensor.matmul(pb[:],
                                                     YVH[fc * 2 + loc // 4][:, (loc % 4) * 128:(loc % 4 + 1) * 128],
                                                     RISB[:, fc * 256:(fc + 1) * 256],
                                                     start=False, stop=(fc == 1))
                            sPE.inc(inst)
                            iv_mm[(w, ci, ab)] = sPE.v
                            if cil == 3 and ab == 1:
                                roll["yfree"] = sPE.v

                        @VEC
                        def _(vector, w=w, half=half, ci=ci, cil=cil, ab=ab,
                              sg_plan=sg_plan):
                            vector.wait_ge(sPE.h, iv_mm[(w, ci, ab)])
                            if cil == 0 and ab == 0:
                                km = sg_mk.get(("last",))
                                if km:
                                    vector.wait_ge(sAC.h, km)
                            if cil >= 2 and ab == 0:
                                vector.wait_ge(sAC.h, sg_plan[cil - 2])
                            sl = cil % 2
                            bias = (BACOL if ab == 0 else BBCOL)[:, ci:ci + 1]
                            inst = vector.tensor_scalar(
                                out=CAB[:, sl * 512 + ab * 256:sl * 512 + (ab + 1) * 256],
                                in0=pIV[ab][:],
                                scalar1=bias, scalar2=None, op0=ALU.add)
                            sDV.inc(inst)
                            ca_ev[(w, ci, ab)] = sDV.v
                            ca_ev[("bank", ab)] = sDV.v

                    @VEC
                    def _(vector, w=w, half=half, cil=cil, sg_plan=sg_plan):
                        ci = half * 4 + cil
                        vector.wait_ge(sAC.h, sg_plan[cil])
                        if half == 0 and cil == 0:
                            km = op_mm.get(("gated", w - 1))
                            if km:
                                vector.wait_ge(sPE.h, km)
                        sl = cil % 2
                        vector.tensor_tensor(
                            out=GATED(ci * 256, 256),
                            in0=CAB[:, sl * 512:sl * 512 + 256],
                            in1=SGV[cil][:, 256:512], op=ALU.mult)
                        inst = vector.tensor_tensor(
                            out=GATED((8 + ci) * 256, 256),
                            in0=CAB[:, sl * 512 + 256:sl * 512 + 512],
                            in1=SGV[cil][:, 0:256], op=ALU.mult)
                        sDV.inc(inst)
                        gat_mk[(w, half)] = sDV.v
                        gat_mk[("last",)] = sDV.v

                @SCA
                def _(scalar, w=w, half=half, sg_plan=sg_plan, prev_gat=prev_gat):
                    if prev_gat:
                        scalar.wait_ge(sDV.h, prev_gat)
                    for cil in range(4):
                        scalar.wait_ge(sDV.h, ca_ev[(w, half * 4 + cil, 1)])
                        inst = scalar.activation(
                            out=SGV[cil],
                            in_=CAB[:, (cil % 2) * 512:(cil % 2 + 1) * 512],
                            func=AF.Sigmoid)
                        sAC.inc(inst)
                        assert sAC.v == sg_plan[cil]
                    sg_mk[(w, half)] = sAC.v
                    sg_mk[("last",)] = sAC.v


            for hc in range(NCC):
                @TEN
                def _(tensor, w=w, hc=hc):
                    if hc == 0:
                        if w == 0:
                            tensor.wait_ge(sW.h, aw_done)
                        for hf in range(2):
                            tensor.wait_ge(sDV.h, gat_mk[(w, hf)])
                    km = op_ev.get(("bank",))
                    if km:
                        tensor.wait_ge(sDV.h, km)
                    for ec in range(16):
                        inst = tensor.matmul(pOP[:],
                                             WOUTT[:, ec * C + hc * 128:ec * C + (hc + 1) * 128],
                                             GATED(ec * 256, 256),
                                             start=(ec == 0), stop=(ec == 15))
                    sPE.inc(inst)
                    op_mm[(w, hc)] = sPE.v
                    if hc == NCC - 1:
                        op_mm[("gated", w)] = sPE.v

                @VEC
                def _(vector, w=w, hc=hc):
                    vector.wait_ge(sPE.h, op_mm[(w, hc)])
                    km = at_mm.get(("atsb",))
                    if km:
                        vector.wait_ge(sPE.h, km)
                    inst = vector.tensor_scalar(out=ATSB[:], in0=pOP[:],
                                                scalar1=BOCOL[:, hc:hc + 1],
                                                scalar2=None, op0=ALU.add)
                    sDV.inc(inst)
                    op_ev[(w, hc)] = sDV.v
                    op_ev[("bank",)] = sDV.v

                @TEN
                def _(tensor, w=w, hc=hc):
                    tensor.wait_ge(sDV.h, op_ev[(w, hc)])
                    km = xadd.get(("bank",))
                    if km:
                        tensor.wait_ge(sDV.h, km)
                    for th in range(2):
                        inst = tensor.transpose(pAT[:, th * 128:(th + 1) * 128],
                                                ATSB[:, th * 128:(th + 1) * 128], IDF[:])
                    sPE.inc(inst)
                    at_mm[(w, hc)] = sPE.v
                    at_mm[("atsb",)] = sPE.v

                @VEC
                def _(vector, w=w, hc=hc):
                    vector.wait_ge(sPE.h, at_mm[(w, hc)])
                    for th in range(2):
                        i = 2 * w + th
                        xs = X[:, i * C + hc * 128:i * C + (hc + 1) * 128]
                        inst = vector.tensor_tensor(out=xs,
                                                    in0=pAT[:, th * 128:(th + 1) * 128],
                                                    in1=xs, op=ALU.add)
                        sDV.inc(inst)
                    xadd[(w, hc)] = sDV.v
                    xadd[("bank",)] = sDV.v

        for i in range(NT):
            emit_ztile(i)
            if i % 2 == 1:
                w = (i - 1) // 2
                emit_window(w)
                fwd_done_w[w] = max(v for k, v in fwd_mm.items() if k[0] == w)
        actx.close()
        return xadd[("bank",)]

    last_comb = emit_ffn(0)
    if DBG_PHASES >= 2:
        barrier()
        last_comb = emit_attn()
    if DBG_PHASES >= 3:
        barrier()
        last_comb = emit_ffn(1)

    @SYN
    def _(sync):
        sync.wait_ge(sDV.h, last_comb)
        inst = sync.dma_start(out=dout.rearrange("(i p) c -> p i c", p=128),
                              in_=X[:].rearrange("p (i c) -> p i c", i=NT))
        sST.inc(inst, 16)
        sync.wait_ge(sST.h, sST.v)

    blk.__exit__(None, None, None)
    ctx.close()
    return nc


def _dft_consts():
    t = np.arange(NFFT)
    f = np.arange(256)
    ang = 2 * np.pi * np.outer(t, f) / NFFT
    rc = np.cos(ang)
    rs = np.sin(ang)
    rs[:, 0] = np.cos(np.pi * t)
    tau = np.arange(256) + 256
    angi = 2 * np.pi * np.outer(f, tau) / NFFT
    ric = (2.0 / NFFT) * np.cos(angi)
    ric[0, :] = 1.0 / NFFT
    ris = (2.0 / NFFT) * np.sin(angi)
    ris[0, :] = ((-1.0) ** np.arange(256)) / NFFT
    return (rc.astype(BF), rs.astype(BF), ric.astype(BF), ris.astype(BF))


def _col128(v):
    n = v.shape[0] // 128
    return np.ascontiguousarray(v.reshape(n, 128).T)


def kernel(L0_attn_x, L0_attn_w_in, L0_attn_b_in, L0_attn_w_a, L0_attn_b_a,
           L0_attn_w_b, L0_attn_b_b, L0_attn_w_out, L0_attn_b_out,
           L0_ffn_router_w, L0_ffn_router_b, L0_ffn_w_c_fc, L0_ffn_b_c_fc,
           L0_ffn_w_gate, L0_ffn_b_gate, L0_ffn_w_c_proj, L0_ffn_b_c_proj):
    f32 = np.float32
    x = np.ascontiguousarray(np.asarray(L0_attn_x, f32))
    rc, rs, ric, ris = _dft_consts()
    base = {
        "rw_col": np.ascontiguousarray(
            np.asarray(L0_ffn_router_w, f32).reshape(NCC, 128, E)
            .transpose(1, 0, 2).reshape(128, NCC * E)),
        "rb_row": np.asarray(L0_ffn_router_b, f32).reshape(1, E),
        "iota_e": np.ascontiguousarray(
            np.broadcast_to(np.arange(E, dtype=f32), (128, E))),
        "tri_bf": np.triu(np.ones((128, 128), f32), 1),
        "ones_bf": np.ones((128, 128), f32),
        "wfc": np.ascontiguousarray(
            np.asarray(L0_ffn_w_c_fc, f32).reshape(E, NCC, 128, NHG, HGRP)
            .transpose(0, 3, 2, 1, 4).reshape(E, NHG, 128, NCC * HGRP)),
        "wg": np.ascontiguousarray(
            np.asarray(L0_ffn_w_gate, f32).reshape(E, NCC, 128, NHG, HGRP)
            .transpose(0, 3, 2, 1, 4).reshape(E, NHG, 128, NCC * HGRP)),
        "wp": np.ascontiguousarray(
            np.asarray(L0_ffn_w_c_proj, f32).reshape(E, NHC, 128, NCC, 128)
            .transpose(0, 3, 2, 1, 4).reshape(E, NCC, 128, NHC * 128)),
        "bfc_col": np.ascontiguousarray(
            np.asarray(L0_ffn_b_c_fc, f32).reshape(E, NHC, 128)
            .transpose(2, 0, 1).reshape(128, E * NHC)),
        "bg_col": np.ascontiguousarray(
            np.asarray(L0_ffn_b_gate, f32).reshape(E, NHC, 128)
            .transpose(2, 0, 1).reshape(128, E * NHC)),
        "bp_col": np.ascontiguousarray(
            np.asarray(L0_ffn_b_c_proj, f32).reshape(E, NCC, 128)
            .transpose(2, 0, 1).reshape(128, E * NCC)),
        "w_inT": np.ascontiguousarray(np.asarray(L0_attn_w_in, f32).T),
        "b_in_row": np.asarray(L0_attn_b_in, f32).reshape(1, 2 * C),
        "w_outT": np.ascontiguousarray(np.asarray(L0_attn_w_out, f32).T),
        "b_out_col": _col128(np.asarray(L0_attn_b_out, f32)),
        "wab_rev": np.ascontiguousarray(np.stack([
            np.asarray(L0_attn_w_a, f32)[:, 0, ::-1].T,
            np.asarray(L0_attn_w_b, f32)[:, 0, ::-1].T])),
        "b_a_col": _col128(np.asarray(L0_attn_b_a, f32)),
        "b_b_col": _col128(np.asarray(L0_attn_b_b, f32)),
        "rc": rc, "rs": rs, "ric": ric, "ris": ris,
    }
    nc = build_nc()
    in_maps = [dict(base, x=np.ascontiguousarray(x[b])) for b in range(8)]
    res = run_bass_kernel_spmd(nc, in_maps, core_ids=list(range(8)))
    return np.stack([np.asarray(r["out"]) for r in res.results], axis=0)


# version 07
# speedup vs baseline: 1.4127x; 1.0431x over previous
import numpy as np
import ml_dtypes
from contextlib import ExitStack

import concourse.bass as bass
import concourse.mybir as mybir
from concourse.bass_utils import run_bass_kernel_spmd
from concourse.bass_core_rust import IndirectOffsetOnAxis

F32 = mybir.dt.float32
BF16 = mybir.dt.bfloat16
I32 = mybir.dt.int32
U32 = mybir.dt.uint32
AF = mybir.ActivationFunctionType
ALU = mybir.AluOpType
AX = mybir.AxisListType.X
BF = ml_dtypes.bfloat16

T, C, H4, E = 2048, 1024, 4096, 8
CAP, CAPP = 614, 640
NT, NCC, NHC = T // 128, C // 128, H4 // 128
NFFT, NW = 512, T // 256
HGRP = 256
NHG = H4 // HGRP
DBG_PHASES = 3
BIG = float(1 << 20)
EPS = 1e-5
SGRPS = [(0, 307), (307, 307)]


class Sem:
    __slots__ = ("h", "v")

    def __init__(self, ctx, nc, name):
        self.h = ctx.enter_context(nc.semaphore(name))
        self.v = 0

    def inc(self, inst, n=1):
        inst.then_inc(self.h, n)
        self.v += n
        return self.v


def build_nc():
    nc = bass.Bass("TRN2", target_bir_lowering=False)

    d = {}
    def din(name, shape, dt=F32):
        d[name] = nc.dram_tensor(name, shape, dt, kind="ExternalInput").ap()
        return d[name]

    dx = din("x", [T, C])
    din("rw_col", [128, NCC * E])
    din("rb_row", [1, E])
    din("iota_e", [128, E])
    din("tri_bf", [128, 128])
    din("ones_bf", [128, 128])
    din("wfc", [E, NHG, 128, NCC * HGRP])
    din("wg", [E, NHG, 128, NCC * HGRP])
    din("wp", [E, NCC, 128, NHC * 128])
    din("bfc_col", [128, E * NHC])
    din("bg_col", [128, E * NHC])
    din("bp_col", [128, E * NCC])
    din("w_inT", [C, 2 * C])
    din("b_in_row", [1, 2 * C])
    din("w_outT", [2 * C, C])
    din("b_out_col", [128, NCC])
    din("wab_rev", [2, 256, C])
    din("b_a_col", [128, NCC])
    din("b_b_col", [128, NCC])
    din("rc", [NFFT, 256], BF16)
    din("rs", [NFFT, 256], BF16)
    din("ric", [256, 256], BF16)
    din("ris", [256, 256], BF16)
    dout = nc.dram_tensor("out", [T, C], F32, kind="ExternalOutput").ap()
    dxrows = nc.dram_tensor("xrows", [E * CAPP + 1, C], BF16, kind="Internal").ap()
    dorows = nc.dram_tensor("orows", [E * CAPP, C], BF16, kind="Internal").ap()

    ctx = ExitStack()
    S = lambda name, shape, dt=F32: ctx.enter_context(nc.sbuf_tensor(name, shape, dt))

    X = S("X", [128, NT * C])
    XBF32 = S("XBF32", [128, NT * C], BF16)
    MSILA = S("MSILA", [128, NHC * CAP], BF16)
    _msf = MSILA[:].bitcast(F32)
    TRI = _msf[:, 8192:8320]
    ONESC = _msf[:, 8320:8448]
    IDB = S("IDB", [128, 128], BF16)
    IDF = S("IDF", [128, 128], F32)
    ONESB = S("ONESB", [1, 128], BF16)
    ONESF = S("ONESF", [1, 128], F32)
    SCRA = S("SCRA", [128, 128], F32)
    SCRB = S("SCRB", [128, 128], F32)
    SMALL = S("SMALL", [128, 1036], F32)
    _sm = [0]

    def sm(n, dt=F32):
        a = _sm[0]
        _sm[0] += n
        assert _sm[0] <= 1036
        v = SMALL[:, a:a + n]
        return v if dt is F32 else v.bitcast(dt)

    IOTAE = sm(E)
    RWC = sm(NCC * E)
    MEANS = sm(NT)
    VARS = sm(NT)
    NEGM = sm(NT)
    RSTD = sm(NT)
    MXA = sm(E)
    ECOLS = sm(2 * NT)
    DSUB = sm(NT)
    P0T = sm(NT)
    HP0 = sm(NT)
    HP1 = sm(NT)
    OH = sm(NT * 16)
    CSTA = sm(NT * 16)
    TOTS = sm(16)
    PB = sm(256)
    RBR = S("RBR", [1, E], F32)
    IXA = S("IXA", [128, E], U32)
    SIDX = S("SIDX", [128, 2 * NT], I32)
    RIDX = S("RIDX", [128, 2 * NT], I32)
    OSBA = S("OSBA", [128, 2 * CAP], BF16)
    XNT = S("XNT", [128, C])
    XNTT = S("XNTT", [128, C])
    WFCB = [S("WFCB0", [128, NCC * HGRP], BF16), S("WFCB1", [128, NCC * HGRP], BF16)]
    WGB = [S("WGB0", [128, NCC * HGRP], BF16), S("WGB1", [128, NCC * HGRP], BF16)]
    WPB = [S("WPB0", [128, NHC * 128], BF16), S("WPB1", [128, NHC * 128], BF16)]
    OROW0 = S("OROW0", [128, C], BF16)
    MSB = [S("MSB0", [128, 320], F32), S("MSB1", [128, 320], F32)]
    OSBB = S("OSBB", [128, 3 * CAP], BF16)
    OSBC = S("OSBC", [128, 3 * CAP], BF16)
    BFCC = S("BFCC", [128, E * NHC], F32)
    BGC = S("BGC", [128, E * NHC], F32)
    BPC = S("BPC", [128, E * NCC], F32)
    DUM = S("DUM", [1, 16], F32)

    sX = Sem(ctx, nc, "sX")
    sW = Sem(ctx, nc, "sW")
    sSC = Sem(ctx, nc, "sSC")
    sER = Sem(ctx, nc, "sER")
    sOR = Sem(ctx, nc, "sOR")
    sGA = Sem(ctx, nc, "sGA")
    sPE = Sem(ctx, nc, "sPE")
    sDV = Sem(ctx, nc, "sDV")
    sAC = Sem(ctx, nc, "sAC")
    sPO = Sem(ctx, nc, "sPO")
    sST = Sem(ctx, nc, "sST")
    sBAR = Sem(ctx, nc, "sBAR")

    blk = nc.Block().__enter__()
    TEN, VEC, SCA, GPS, SYN = blk.tensor, blk.vector, blk.scalar, blk.gpsimd, blk.sync

    @SYN
    def _(sync):
        sync.dma_start(out=X[:].rearrange("p (i c) -> p i c", i=NT),
                       in_=dx.rearrange("(i p) c -> p i c", p=128)).then_inc(sX.h, 16)
        sX.v += 16
        for name, dst in [("rw_col", RWC), ("rb_row", RBR), ("iota_e", IOTAE),
                          ("tri_bf", TRI), ("ones_bf", ONESC),
                          ("bfc_col", BFCC), ("bg_col", BGC), ("bp_col", BPC)]:
            sync.dma_start(out=dst[:], in_=d[name][:]).then_inc(sX.h, 16)
            sX.v += 16
    setup_loads = sX.v

    @GPS
    def _(gpsimd):
        gpsimd.memset(IDB[:], 0.0)
        gpsimd.affine_select(out=IDB[:], in_=IDB[:], compare_op=ALU.not_equal,
                             fill=1.0, base=0, pattern=[[-1, 128]], channel_multiplier=1)
        gpsimd.memset(IDF[:], 0.0)
        gpsimd.affine_select(out=IDF[:], in_=IDF[:], compare_op=ALU.not_equal,
                             fill=1.0, base=0, pattern=[[-1, 128]], channel_multiplier=1)
        gpsimd.memset(ONESB[:], 1.0)
        gpsimd.memset(DUM[:], 0.0)
        inst = gpsimd.memset(ONESF[:], 1.0)
        sPO.inc(inst)
    setup_po = sPO.v

    def barrier():
        for dec, kind in ((TEN, "n"), (VEC, "v"), (SCA, "s"), (GPS, "v"), (SYN, "n")):
            @dec
            def _(eng, kind=kind):
                if kind == "v":
                    inst = eng.memset(DUM[0:1, 0:4], 0.0)
                elif kind == "s":
                    inst = eng.activation(out=DUM[0:1, 4:8], in_=DUM[0:1, 0:4], func=AF.Copy)
                else:
                    inst = eng.nop()
                inst.then_inc(sBAR.h, 1)
        sBAR.v += 5
        tgt = sBAR.v
        for dec in (TEN, VEC, SCA, GPS, SYN):
            @dec
            def _(eng):
                eng.wait_ge(sBAR.h, tgt)

    def emit_ln(first=False, scratch=None):
        sq_pl = {i: sAC.v + i + 1 for i in range(NT)}
        var_done = {}
        for i in range(NT):
            @VEC
            def _(vector, i=i):
                if first and i == 0:
                    vector.wait_ge(sX.h, setup_loads)
                    vector.wait_ge(sPO.h, setup_po)
                xt = X[:, i * C:(i + 1) * C]
                inst = vector.reduce_sum(out=MEANS[:, i:i + 1], in_=xt, axis=AX)
                sDV.inc(inst)
                vector.wait_ge(sDV.h, sDV.v)
                inst = vector.tensor_scalar(out=NEGM[:, i:i + 1],
                                            in0=MEANS[:, i:i + 1],
                                            scalar1=-1.0 / C, scalar2=None,
                                            op0=ALU.mult)
                sDV.inc(inst)
                vector.wait_ge(sDV.h, sDV.v)
                inst = vector.tensor_scalar(out=scratch[:], in0=xt,
                                            scalar1=NEGM[:, i:i + 1],
                                            scalar2=None, op0=ALU.add)
                sDV.inc(inst)
                vector.wait_ge(sDV.h, sDV.v)
                inst = vector.tensor_tensor(out=scratch[:], in0=scratch[:],
                                            in1=scratch[:], op=ALU.mult)
                sDV.inc(inst)
                vector.wait_ge(sDV.h, sDV.v)
                inst = vector.reduce_sum(out=VARS[:, i:i + 1], in_=scratch[:],
                                         axis=AX)
                sDV.inc(inst)
                vector.wait_ge(sDV.h, sDV.v)
                inst = vector.tensor_scalar(out=VARS[:, i:i + 1],
                                            in0=VARS[:, i:i + 1],
                                            scalar1=1.0 / C, scalar2=float(EPS),
                                            op0=ALU.mult, op1=ALU.add)
                sDV.inc(inst)
                var_done[i] = sDV.v

            @VEC
            def _(vector, i=i):
                vector.wait_ge(sAC.h, sq_pl[i])
                inst = vector.reciprocal(out=RSTD[:, i:i + 1], in_=P0T[:, i:i + 1])
                sDV.inc(inst)
                vector.wait_ge(sDV.h, sDV.v)

        @SCA
        def _(scalar):
            for i in range(NT):
                scalar.wait_ge(sDV.h, var_done[i])
                inst = scalar.activation(out=P0T[:, i:i + 1], in_=VARS[:, i:i + 1],
                                         func=AF.Sqrt)
                sAC.inc(inst)
                assert sAC.v == sq_pl[i]
        return sDV.v

    def emit_ffn(call):
        tag = f"f{call}"

        rctx = ExitStack()
        rS = lambda n, shape, dt=F32: rctx.enter_context(nc.sbuf_tensor(f"{n}_{tag}", shape, dt))
        rP = lambda n, shape, dt=F32: rctx.enter_context(nc.psum_tensor(f"{n}_{tag}", shape, dt))
        XNBF = XBF32
        pTR = [rP("pTR0", [128, 512]), rP("pTR1", [128, 512])]
        pLG = [rP("pLG0", [128, E]), rP("pLG1", [128, E])]
        pCS = [rP("pCS0", [128, 16]), rP("pCS1", [128, 16])]
        pTOT = rP("pTOT", [128, 16])

        if call == 1:
            @SYN
            def _(sync):
                for name, dst in [("tri_bf", TRI), ("ones_bf", ONESC),
                                  ("bfc_col", BFCC), ("bg_col", BGC)]:
                    inst = sync.dma_start(out=dst[:], in_=d[name][:])
                    sX.inc(inst, 16)
        reload_mark = sX.v

        emit_ln(first=(call == 0), scratch=XNT[:])

        xnf_done, xnbf_done, trev_done, lgev_done = {}, {}, {}, {}
        tr_mm, lg_mm, cs_mm = {}, {}, {}
        oh_done, csev_done = {}, {}

        for i in range(NT):
            @VEC
            def _(vector, i=i):
                if i >= 1:
                    vector.wait_ge(sPE.h, tr_mm[(i - 1, 1)])
                inst = vector.tensor_scalar(
                    out=XNT[:],
                    in0=X[:, i * C:(i + 1) * C],
                    scalar1=NEGM[:, i:i + 1], scalar2=RSTD[:, i:i + 1],
                    op0=ALU.add, op1=ALU.mult)
                sDV.inc(inst)
                xnf_done[i] = sDV.v
                vector.wait_ge(sDV.h, sDV.v)
                inst = vector.tensor_copy(out=XNBF[:, i * C:(i + 1) * C],
                                          in_=XNT[:])
                sDV.inc(inst)
                xnbf_done[i] = sDV.v

            for half in range(2):
                @TEN
                def _(tensor, i=i, half=half):
                    tensor.wait_ge(sDV.h, xnf_done[i])
                    k = i * 2 + half
                    if k >= 2:
                        tensor.wait_ge(sDV.h, trev_done[k - 2])
                    for j in range(4):
                        cc = half * 4 + j
                        inst = tensor.transpose(
                            pTR[half][:, j * 128:(j + 1) * 128],
                            XNT[:, cc * 128:(cc + 1) * 128],
                            IDF[:])
                    sPE.inc(inst)
                    tr_mm[(i, half)] = sPE.v

                @VEC
                def _(vector, i=i, half=half):
                    vector.wait_ge(sPE.h, tr_mm[(i, half)])
                    if half == 0 and i >= 1:
                        vector.wait_ge(sPE.h, lg_mm[i - 1])
                    inst = vector.tensor_copy(
                        out=XNTT[:, half * 512:(half + 1) * 512],
                        in_=pTR[half][:])
                    sDV.inc(inst)
                    trev_done[i * 2 + half] = sDV.v

            @TEN
            def _(tensor, i=i):
                if i == 0 and call == 0:
                    tensor.wait_ge(sX.h, setup_loads)
                tensor.wait_ge(sDV.h, trev_done[i * 2 + 1])
                if i >= 2:
                    tensor.wait_ge(sDV.h, lgev_done[i - 2])
                pb = pLG[i % 2]
                for cc in range(NCC):
                    tensor.matmul(pb[:],
                                  XNTT[:, cc * 128:(cc + 1) * 128],
                                  RWC[:, cc * E:(cc + 1) * E],
                                  start=(cc == 0), stop=False)
                inst = tensor.matmul(pb[:], ONESF[0:1, :], RBR[0:1, :],
                                     start=False, stop=True)
                sPE.inc(inst)
                lg_mm[i] = sPE.v

            @VEC
            def _(vector, i=i):
                vector.wait_ge(sPE.h, lg_mm[i])
                mx = MXA[:]
                inst = vector.max(out=mx, in_=pLG[i % 2][:])
                sDV.inc(inst)
                vector.wait_ge(sDV.h, sDV.v)
                inst = vector.max_index(out=IXA[:], in_max=mx, in_values=pLG[i % 2][:])
                sDV.inc(inst)
                lgev_done[i] = sDV.v
                vector.wait_ge(sDV.h, sDV.v)
                inst = vector.tensor_copy(out=ECOLS[:, 2 * i:2 * i + 2], in_=IXA[:, 0:2])
                sDV.inc(inst)
                vector.tensor_tensor(out=DSUB[:, i:i + 1], in0=mx[:, 0:1],
                                     in1=mx[:, 1:2], op=ALU.subtract)
                vector.wait_ge(sDV.h, sDV.v)
                vector.tensor_scalar(out=OH[:, i * 16:i * 16 + 8], in0=IOTAE[:],
                                     scalar1=ECOLS[:, 2 * i:2 * i + 1], scalar2=None,
                                     op0=ALU.is_equal)
                inst = vector.tensor_scalar(out=OH[:, i * 16 + 8:i * 16 + 16], in0=IOTAE[:],
                                            scalar1=ECOLS[:, 2 * i + 1:2 * i + 2],
                                            scalar2=None, op0=ALU.is_equal)
                sDV.inc(inst)
                oh_done[i] = sDV.v

        for i in range(NT):
            @TEN
            def _(tensor, i=i):
                tensor.wait_ge(sDV.h, oh_done[i])
                if i == 0 and call == 1:
                    tensor.wait_ge(sX.h, reload_mark)
                if i >= 2:
                    tensor.wait_ge(sDV.h, csev_done[i - 2])
                pb = pCS[i % 2]
                for j in range(i):
                    tensor.matmul(pb[:], ONESC[:], OH[:, j * 16:(j + 1) * 16],
                                  start=(j == 0), stop=False)
                inst = tensor.matmul(pb[:], TRI[:], OH[:, i * 16:(i + 1) * 16],
                                     start=(i == 0), stop=True)
                sPE.inc(inst)
                cs_mm[i] = sPE.v

            @VEC
            def _(vector, i=i):
                vector.wait_ge(sPE.h, cs_mm[i])
                inst = vector.tensor_copy(out=CSTA[:, i * 16:(i + 1) * 16],
                                          in_=pCS[i % 2][:])
                sDV.inc(inst)
                csev_done[i] = sDV.v

        @TEN
        def _(tensor):
            tensor.wait_ge(sDV.h, oh_done[NT - 1])
            for j in range(NT):
                inst = tensor.matmul(pTOT[:], ONESC[:], OH[:, j * 16:(j + 1) * 16],
                                     start=(j == 0), stop=(j == NT - 1))
            sPE.inc(inst)
            cs_mm["tots"] = sPE.v

        @VEC
        def _(vector):
            vector.wait_ge(sPE.h, cs_mm["tots"])
            inst = vector.tensor_copy(out=TOTS[:], in_=pTOT[:])
            sDV.inc(inst)
        tots_ev = sDV.v

        P_POS = PB[:, 0:32]
        P_SCL = PB[:, 32:64]
        P_DM = PB[:, 64:96]
        P_PC = PB[:, 96:128]
        P_T1 = PB[:, 128:160]
        P_T2 = PB[:, 160:192]

        @VEC
        def _(vector):
            vector.wait_ge(sDV.h, tots_ev)
            for i in range(NT):
                inst = vector.tensor_tensor(
                    out=CSTA[:, i * 16 + 8:i * 16 + 16],
                    in0=CSTA[:, i * 16 + 8:i * 16 + 16],
                    in1=TOTS[:, 0:8], op=ALU.add)
                sDV.inc(inst)
            vector.wait_ge(sDV.h, sDV.v)
            inst = vector.tensor_tensor(out=CSTA[:], in0=OH[:], in1=CSTA[:],
                                        op=ALU.mult)
            sDV.inc(inst)
            inst = vector.tensor_scalar(out=P_SCL, in0=ECOLS[:],
                                        scalar1=float(CAPP), scalar2=None,
                                        op0=ALU.mult)
            sDV.inc(inst)
            vector.wait_ge(sDV.h, sDV.v)
            inst = vector.reduce_sum(
                out=P_POS, in_=CSTA[:].rearrange("p (t e) -> p t e", t=32), axis=AX)
            sDV.inc(inst)
            vector.wait_ge(sDV.h, sDV.v)
            inst = vector.tensor_scalar(out=P_DM, in0=P_POS, scalar1=613.5,
                                        scalar2=BIG, op0=ALU.is_gt, op1=ALU.mult)
            sDV.inc(inst)
            inst = vector.tensor_scalar(out=P_PC, in0=P_POS, scalar1=613.0,
                                        scalar2=None, op0=ALU.min)
            sDV.inc(inst)
            inst = vector.tensor_tensor(out=P_T1, in0=P_SCL, in1=P_POS, op=ALU.add)
            sDV.inc(inst)
            vector.wait_ge(sDV.h, sDV.v)
            inst = vector.tensor_tensor(out=P_T2, in0=P_T1, in1=P_DM, op=ALU.add)
            sDV.inc(inst)
            vector.wait_ge(sDV.h, sDV.v)
            inst = vector.tensor_scalar(out=P_T2, in0=P_T2,
                                        scalar1=float(E * CAPP), scalar2=None,
                                        op0=ALU.min)
            sDV.inc(inst)
            inst = vector.tensor_tensor(out=P_T1, in0=P_SCL, in1=P_PC, op=ALU.add)
            sDV.inc(inst)
            vector.wait_ge(sDV.h, sDV.v)
            inst = vector.tensor_copy(out=SIDX[:], in_=P_T2)
            sDV.inc(inst)
            inst = vector.tensor_copy(out=RIDX[:], in_=P_T1)
            sDV.inc(inst)
            vector.wait_ge(sDV.h, sDV.v)
        all_idx = sDV.v

        for i in range(NT):
            @GPS
            def _(gpsimd, i=i):
                if i == 0:
                    gpsimd.wait_ge(sDV.h, all_idx)
                for k in range(2):
                    inst = gpsimd.indirect_dma_start(
                        out=dxrows,
                        out_offset=IndirectOffsetOnAxis(SIDX[:, 2 * i + k:2 * i + k + 1], 0),
                        in_=XNBF[:, i * C:(i + 1) * C], in_offset=None)
                    sSC.inc(inst, 16)
        scatter_done = sSC.v

        @SCA
        def _(scalar):
            scalar.wait_ge(sDV.h, all_idx)
            inst = scalar.activation(out=P0T[:], in_=DSUB[:], func=AF.Sigmoid)
            sAC.inc(inst)
        sig_done = sAC.v

        @VEC
        def _(vector):
            vector.wait_ge(sAC.h, sig_done)
            vector.tensor_scalar(out=HP0[:], in0=P0T[:], scalar1=0.5, scalar2=None,
                                 op0=ALU.mult)
            inst = vector.tensor_scalar(out=HP1[:], in0=P0T[:], scalar1=-0.5,
                                        scalar2=0.5, op0=ALU.mult, op1=ALU.add)
            sDV.inc(inst)
        hp_done = sDV.v

        rctx.close()

        pctx = ExitStack()
        pS = lambda n, shape, dt=F32: pctx.enter_context(nc.sbuf_tensor(f"{n}_{tag}", shape, dt))
        pP = lambda n, shape, dt=F32: pctx.enter_context(nc.psum_tensor(f"{n}_{tag}", shape, dt))
        XSTG = XBF32[:, 0:5 * C]
        EINPT = [XBF32[:, 5 * C:10 * C], XBF32[:, 10 * C:15 * C]]
        MSIL = MSILA
        def osb(cc, a, b):
            if cc < 2:
                return OSBA[:, cc * CAP + a:cc * CAP + b]
            if cc < 5:
                return OSBB[:, (cc - 2) * CAP + a:(cc - 2) * CAP + b]
            return OSBC[:, (cc - 5) * CAP + a:(cc - 5) * CAP + b]
        OROW = [OROW0, XBF32[:, 15 * C:16 * C]]
        GROW = [OROW0, XBF32[:, 15 * C:16 * C],
                XBF32[:, 10 * C:11 * C], XBF32[:, 11 * C:12 * C]]
        pH = [pP("pH0", [128, 320]), pP("pH1", [128, 320])]
        pG = [pP("pG0", [128, 320]), pP("pG1", [128, 320])]
        pPj = [pP("pPj0", [128, 320]), pP("pPj1", [128, 320])]
        pOT = pP("pOT", [128, 1024], BF16)
        pXP = pP("pXP", [128, 1024], BF16)

        wfc_mark, wp_mark, wfc_used, wp_used = {}, {}, {}, {}
        er_mark, einp_used, xstg_free, gmark = {}, {}, {}, {}
        xp_mm, xp_ev = {}, {}
        hg_mm, h_ev, g_mu, sil_mk = {}, {}, {}, {}
        pj_mm, o_ev, otr_mm, orow_ev = {}, {}, {}, {}
        silu_last, proj_last, otr_last = {}, {}, {}
        orow_st_list = []

        for e in range(E):
            @SYN
            def _(sync, e=e):
                if e == 0:
                    sync.wait_ge(sSC.h, scatter_done)
                if e >= 1:
                    sync.wait_ge(sPE.h, xstg_free[e - 1])
                inst = sync.dma_start(
                    out=XSTG.rearrange("p (t c) -> p t c", t=5),
                    in_=dxrows[e * CAPP:e * CAPP + 5 * 128, :]
                    .rearrange("(t p) c -> p t c", p=128))
                sER.inc(inst, 16)
                er_mark[e] = sER.v

            for t in range(5):
                @TEN
                def _(tensor, e=e, t=t):
                    if t == 0:
                        tensor.wait_ge(sER.h, er_mark[e])
                        if e >= 1:
                            tensor.wait_ge(sDV.h, xp_ev[(e - 1, 4)])
                    else:
                        tensor.wait_ge(sDV.h, xp_ev[(e, t - 1)])
                    for cc in range(NCC):
                        inst = tensor.transpose(
                            pXP[:, cc * 128:(cc + 1) * 128],
                            XSTG[:, t * C + cc * 128:t * C + (cc + 1) * 128],
                            IDB[:])
                    sPE.inc(inst)
                    xp_mm[(e, t)] = sPE.v
                    if t == 4:
                        xstg_free[e] = sPE.v

                @VEC
                def _(vector, e=e, t=t):
                    vector.wait_ge(sPE.h, xp_mm[(e, t)])
                    if e >= 2 and t == 0:
                        vector.wait_ge(sPE.h, einp_used[e - 2])
                    inst = vector.tensor_copy(
                        out=EINPT[e % 2]
                        .rearrange("p (cc s) -> p cc s", cc=NCC)[:, :, t * 128:(t + 1) * 128],
                        in_=pXP[:].rearrange("p (cc j) -> p cc j", cc=NCC))
                    sDV.inc(inst)
                    xp_ev[(e, t)] = sDV.v

            for hg in range(NHG):
                _npend = (hg % 2) * 2 * (HGRP // 128)
                for j in range(2 * (HGRP // 128)):
                    sil_mk[(e, hg * 2 * (HGRP // 128) + j)] = sAC.v + _npend + j + 1

                @GPS
                def _(gpsimd, e=e, hg=hg):
                    gidx = e * NHG + hg
                    if gidx >= 2:
                        gpsimd.wait_ge(sPE.h, wfc_used[gidx - 2])
                    buf = gidx % 2
                    inst = gpsimd.dma_start(out=WFCB[buf][:], in_=d["wfc"][e, hg])
                    sW.inc(inst, 16)
                    inst = gpsimd.dma_start(out=WGB[buf][:], in_=d["wg"][e, hg])
                    sW.inc(inst, 16)
                    wfc_mark[e * NHG + hg] = sW.v

                for hsub in range(HGRP // 128):
                    hc = hg * (HGRP // 128) + hsub
                    for gi, (s0, sn) in enumerate(SGRPS):
                        q = hc * 2 + gi

                        @TEN
                        def _(tensor, e=e, hg=hg, hc=hc, gi=gi, s0=s0, sn=sn, q=q):
                            ebuf = EINPT[e % 2]
                            buf = (e * NHG + hg) % 2
                            pi = q % 2
                            if q == 0:
                                tensor.wait_ge(sDV.h, xp_ev[(e, 4)])
                            if gi == 0 and hc % (HGRP // 128) == 0:
                                tensor.wait_ge(sW.h, wfc_mark[e * NHG + hg])
                            if q >= 2:
                                wk = (e, q - 2)
                            elif e > 0:
                                wk = (e - 1, 2 * NHC - 2 + q)
                            else:
                                wk = None
                            if wk is not None:
                                tensor.wait_ge(sDV.h, g_mu[wk])
                            hof = hc % (HGRP // 128)
                            for cc in range(NCC):
                                tensor.matmul(
                                    pH[pi][:, 0:sn],
                                    WFCB[buf][:, cc * HGRP + hof * 128:cc * HGRP + (hof + 1) * 128],
                                    ebuf[:, cc * CAPP + s0:cc * CAPP + s0 + sn],
                                    start=(cc == 0), stop=(cc == NCC - 1))
                            for cc in range(NCC):
                                inst = tensor.matmul(
                                    pG[pi][:, 0:sn],
                                    WGB[buf][:, cc * HGRP + hof * 128:cc * HGRP + (hof + 1) * 128],
                                    ebuf[:, cc * CAPP + s0:cc * CAPP + s0 + sn],
                                    start=(cc == 0), stop=(cc == NCC - 1))
                            sPE.inc(inst)
                            hg_mm[(e, q)] = sPE.v
                            if gi == 1 and hof == (HGRP // 128) - 1:
                                wfc_used[e * NHG + hg] = sPE.v
                            if q == 2 * NHC - 1:
                                einp_used[e] = sPE.v

                        @VEC
                        def _(vector, e=e, hc=hc, s0=s0, sn=sn, q=q):
                            pi = q % 2
                            vector.wait_ge(sPE.h, hg_mm[(e, q)])
                            if call == 1 and e == 0 and q == 0:
                                vector.wait_ge(sX.h, reload_mark)
                            if q >= 2:
                                vector.wait_ge(sAC.h, sil_mk[(e, q - 2)])
                            elif e > 0:
                                vector.wait_ge(sAC.h, sil_mk[(e - 1, 2 * NHC - 2 + q)])
                            inst = vector.tensor_scalar(
                                out=MSB[pi][:, 0:sn], in0=pH[pi][:, 0:sn],
                                scalar1=BFCC[:, e * NHC + hc:e * NHC + hc + 1],
                                scalar2=None, op0=ALU.add)
                            sDV.inc(inst)
                            h_ev[(e, q)] = sDV.v
                            vector.wait_ge(sDV.h, sDV.v)
                            inst = vector.scalar_tensor_tensor(
                                out=MSB[pi][:, 0:sn], in0=pG[pi][:, 0:sn],
                                scalar=BGC[:, e * NHC + hc:e * NHC + hc + 1],
                                in1=MSB[pi][:, 0:sn], op0=ALU.add, op1=ALU.mult)
                            sDV.inc(inst)
                            g_mu[(e, q)] = sDV.v

                if hg % 2 == 1:
                    @SCA
                    def _(scalar, e=e, hg=hg):
                        for hgx in (hg - 1, hg):
                            for hsub in range(HGRP // 128):
                                hc = hgx * (HGRP // 128) + hsub
                                for gi, (s0, sn) in enumerate(SGRPS):
                                    q = hc * 2 + gi
                                    scalar.wait_ge(sDV.h, g_mu[(e, q)])
                                    if e > 0 and q == 0:
                                        scalar.wait_ge(sPE.h, proj_last[e - 1])
                                    inst = scalar.activation(
                                        out=MSIL[:, hc * CAP + s0:hc * CAP + s0 + sn],
                                        in_=MSB[q % 2][:, 0:sn], func=AF.Silu)
                                    sAC.inc(inst)
                                    assert sil_mk[(e, q)] == sAC.v
                                    if q == 2 * NHC - 1:
                                        silu_last[e] = sAC.v

            for cc in range(NCC):
                @GPS
                def _(gpsimd, e=e, cc=cc):
                    cidx = e * NCC + cc
                    if cidx >= 2:
                        gpsimd.wait_ge(sPE.h, wp_used[cidx - 2])
                    inst = gpsimd.dma_start(out=WPB[cidx % 2][:], in_=d["wp"][e, cc])
                    sW.inc(inst, 16)
                    wp_mark[cidx] = sW.v

                for gi, (s0, sn) in enumerate(SGRPS):
                    q = cc * 2 + gi

                    @TEN
                    def _(tensor, e=e, cc=cc, gi=gi, s0=s0, sn=sn, q=q):
                        if q == 0:
                            tensor.wait_ge(sAC.h, silu_last[e])
                        if gi == 0:
                            tensor.wait_ge(sW.h, wp_mark[e * NCC + cc])
                        if q >= 2:
                            tensor.wait_ge(sDV.h, o_ev[(e, q - 2)])
                        elif e > 0:
                            tensor.wait_ge(sDV.h, o_ev[(e - 1, 2 * NCC - 2 + q)])
                        wb = WPB[(e * NCC + cc) % 2]
                        pi = q % 2
                        for hcj in range(NHC):
                            inst = tensor.matmul(
                                pPj[pi][:, 0:sn], wb[:, hcj * 128:(hcj + 1) * 128],
                                MSIL[:, hcj * CAP + s0:hcj * CAP + s0 + sn],
                                start=(hcj == 0), stop=(hcj == NHC - 1))
                        sPE.inc(inst)
                        pj_mm[(e, q)] = sPE.v
                        if gi == 1:
                            wp_used[e * NCC + cc] = sPE.v
                        if q == 2 * NCC - 1:
                            proj_last[e] = sPE.v

                    @VEC
                    def _(vector, e=e, cc=cc, s0=s0, sn=sn, q=q):
                        vector.wait_ge(sPE.h, pj_mm[(e, q)])
                        if e > 0 and q == 0:
                            vector.wait_ge(sPE.h, otr_last[e - 1])
                        pi = q % 2
                        inst = vector.tensor_scalar(
                            out=osb(cc, s0, s0 + sn),
                            in0=pPj[pi][:, 0:sn],
                            scalar1=BPC[:, e * NCC + cc:e * NCC + cc + 1],
                            scalar2=None, op0=ALU.add)
                        sDV.inc(inst)
                        o_ev[(e, q)] = sDV.v

            for sc_i in range(5):
                s0 = sc_i * 128
                sn = min(128, CAP - s0)

                @TEN
                def _(tensor, e=e, sc_i=sc_i, s0=s0, sn=sn):
                    tensor.wait_ge(sDV.h, o_ev[(e, 2 * NCC - 1)])
                    if sc_i >= 1:
                        tensor.wait_ge(sDV.h, orow_ev[(e, sc_i - 1)])
                    elif e > 0:
                        tensor.wait_ge(sDV.h, orow_ev[(e - 1, 4)])
                    for cc in range(NCC):
                        inst = tensor.transpose(
                            pOT[0:sn, cc * 128:(cc + 1) * 128],
                            osb(cc, s0, s0 + sn), IDB[:])
                    sPE.inc(inst)
                    otr_mm[(e, sc_i)] = sPE.v
                    if sc_i == 4:
                        otr_last[e] = sPE.v

                @VEC
                def _(vector, e=e, sc_i=sc_i, sn=sn):
                    vector.wait_ge(sPE.h, otr_mm[(e, sc_i)])
                    if len(orow_st_list) >= 2:
                        vector.wait_ge(sOR.h, orow_st_list[-2])
                    inst = vector.tensor_copy(out=OROW[sc_i % 2][0:sn, :], in_=pOT[0:sn, :])
                    sDV.inc(inst)
                    orow_ev[(e, sc_i)] = sDV.v

                @SCA
                def _(scalar, e=e, sc_i=sc_i, s0=s0, sn=sn):
                    scalar.wait_ge(sDV.h, orow_ev[(e, sc_i)])
                    inst = scalar.dma_start(
                        out=dorows[e * CAPP + s0:e * CAPP + s0 + sn, :],
                        in_=OROW[sc_i % 2][0:sn, :])
                    sOR.inc(inst, 16)
                orow_st_list.append(sOR.v)
        or_all = sOR.v

        comb = {}
        for i in range(NT):
            @GPS
            def _(gpsimd, i=i):
                if i == 0:
                    gpsimd.wait_ge(sOR.h, or_all)
                if i >= 2:
                    gpsimd.wait_ge(sDV.h, comb[i - 2])
                pr = 2 * (i % 2)
                for k in range(2):
                    inst = gpsimd.indirect_dma_start(
                        out=GROW[pr + k][:], out_offset=None,
                        in_=dorows,
                        in_offset=IndirectOffsetOnAxis(RIDX[:, 2 * i + k:2 * i + k + 1], 0))
                    sGA.inc(inst, 16)
            gmark[i] = sGA.v

            @VEC
            def _(vector, i=i):
                if i == 0:
                    vector.wait_ge(sDV.h, hp_done)
                vector.wait_ge(sGA.h, gmark[i])
                pr = 2 * (i % 2)
                xt = X[:, i * C:(i + 1) * C]
                inst = vector.scalar_tensor_tensor(
                    out=xt, in0=GROW[pr][:], scalar=HP0[:, i:i + 1], in1=xt,
                    op0=ALU.mult, op1=ALU.add)
                sDV.inc(inst)
                vector.wait_ge(sDV.h, sDV.v)
                inst = vector.scalar_tensor_tensor(
                    out=xt, in0=GROW[pr + 1][:], scalar=HP1[:, i:i + 1], in1=xt,
                    op0=ALU.mult, op1=ALU.add)
                sDV.inc(inst)
                comb[i] = sDV.v
        pctx.close()
        return comb

    def emit_attn():
        actx = ExitStack()
        aS = lambda n, shape, dt=F32: actx.enter_context(nc.sbuf_tensor(n, shape, dt))
        aP = lambda n, shape, dt=F32: actx.enter_context(nc.psum_tensor(n, shape, dt))

        WINT = XBF32
        WOUTT = MSILA[:, 0:16 * C]
        SPECP = [WFCB[0][:], WFCB[1][:], WGB[0][:], WGB[1][:]]
        ZBUFP = [WPB[0][:, 0:2048], WPB[0][:, 2048:4096],
                 WPB[1][:, 0:2048], WPB[1][:, 2048:4096]]
        GATP0 = aS("GAT0", [128, 2048], BF16)
        GATP1 = aS("GAT1", [128, 1024], BF16)

        def SPEC(c0, n):
            return SPECP[c0 // 2048][:, c0 % 2048:c0 % 2048 + n]
        YU = aS("YU", [128, 2 * 1024], BF16)
        YVH = [BFCC[:].bitcast(BF16)[:, 0:512], BGC[:].bitcast(BF16)[:, 0:512],
               XNT[:].bitcast(BF16)[:, 1024:1536],
               XNTT[:].bitcast(BF16)[:, 1536:2048]]

        CAB = XNT[:].bitcast(BF16)[:, 0:2 * 512]
        XN2T = XNTT[:].bitcast(BF16)[:, 0:C]
        RCB = OSBA[:, 0:1024]
        RSB = OSBB[:, 0:1024]
        RICB = OSBB[:, 1024:1536]
        RISB = OSBC[:, 0:512]
        XNBF2 = OSBC[:, 512:1536]
        ZS = MSILA[:, 16 * C:16 * C + 2048]
        TMW = MSILA[:, 16 * C + 2048:16 * C + 2560]
        TMG = MSILA[:, 16 * C + 2560:16 * C + 3072]
        ATSB = PB[:, 0:256]
        SGV = [MSB[0][:].bitcast(BF16)[:, 0:512],
               MSB[1][:].bitcast(BF16)[:, 0:512],
               OROW0[:, 0:512], OROW0[:, 512:1024]]
        GBH = [GATP1[:, k * 256:(k + 1) * 256] for k in range(4)] + [
            OSBB[:, 1536:1792], OSBC[:, 1536:1792],
            SCRA[:].bitcast(BF16)[:, 0:256], SCRB[:].bitcast(BF16)[:, 0:256]]

        def GATED(c0, n):
            assert n == 256
            ec = c0 // 256
            if ec < 8:
                return GATP0[:, ec * 256:(ec + 1) * 256]
            return GBH[ec - 8]

        BINRV = [OH[:].bitcast(BF16)[0:1, 0:512],
                 CSTA[:].bitcast(BF16)[0:1, 0:512],
                 XNT[:].bitcast(BF16)[0:1, 1536:2048],
                 XNTT[:].bitcast(BF16)[0:1, 1024:1536]]
        BACOL = TOTS[:, 0:8]
        BBCOL = TOTS[:, 8:16]
        BOCOL = MXA[:, 0:8]

        pZ = aP("pZ", [128, 512])
        pXT = aP("pXT", [128, 1024], BF16)
        pFW = [aP("pFW0", [128, 512]), aP("pFW1", [128, 512])]
        pIV = [aP("pIV0", [128, 256]), aP("pIV1", [128, 256])]
        pOP = aP("pOP", [128, 256])
        pAT = aP("pAT", [128, 256])

        @SYN
        def _(sync):
            for dst, name in [(RICB, "ric"), (RISB, "ris")]:
                inst = sync.dma_start(
                    out=dst[:].rearrange("p (f t) -> p f t", f=2),
                    in_=d[name].rearrange("(f p) t -> p f t", p=128))
                sX.inc(inst, 16)
            for dst, name in [(BACOL, "b_a_col"), (BBCOL, "b_b_col"), (BOCOL, "b_out_col")]:
                inst = sync.dma_start(out=dst[:], in_=d[name][:])
                sX.inc(inst, 16)
            inst = sync.dma_start(
                out=RCB[:].rearrange("p (tc f) -> p tc f", tc=4),
                in_=d["rc"].rearrange("(tc p) f -> p tc f", p=128))
            sX.inc(inst, 16)
            inst = sync.dma_start(
                out=RSB[:].rearrange("p (tc f) -> p tc f", tc=4),
                in_=d["rs"].rearrange("(tc p) f -> p tc f", p=128))
            sX.inc(inst, 16)
        aconst = sX.v

        WREV = GATP0[:]

        @GPS
        def _(gpsimd):
            inst = gpsimd.dma_start(
                out=WINT[:].rearrange("p (hc c) -> p hc c", hc=NCC),
                in_=d["w_inT"].rearrange("(hc p) c -> p hc c", p=128))
            sW.inc(inst, 16)
            inst = gpsimd.dma_start(
                out=WOUTT.rearrange("p (ec h) -> p ec h", ec=16),
                in_=d["w_outT"].rearrange("(ec p) h -> p ec h", p=128))
            sW.inc(inst, 16)
            for cg in range(4):
                inst = gpsimd.dma_start(out=BINRV[cg],
                                        in_=d["b_in_row"][:, cg * 512:(cg + 1) * 512])
                sW.inc(inst, 16)
            gpsimd.memset(ZBUFP[0][:], 0.0)
            inst = gpsimd.memset(ZBUFP[1][:], 0.0)
            sPO.inc(inst)
        aw_done = sW.v
        zhist = sPO.v

        spec_mm, spec_ev, wrev_mark = {}, {}, {}
        fwd_mm, fwd_ev, yu_mk, yv_mk = {}, {}, {}, {}
        iv_mm, ca_ev, sg_mk, gat_mk = {}, {}, {}, {}
        op_mm, op_ev, at_mm, xadd = {}, {}, {}, {}
        xnb2_done, xtr_mm, xtr_ev, z_mm, z_ev = {}, {}, {}, {}, {}
        fwd_done_w = {}
        roll = {"fwbank": [None, None], "yfree": None, "slice": {}}

        sbank = [0]
        for filt in range(2):
            @GPS
            def _(gpsimd, filt=filt):
                if filt == 1:
                    gpsimd.wait_ge(sPE.h, spec_mm[(0, "last")])
                inst = gpsimd.dma_start(
                    out=WREV.rearrange("p (kc c) -> p kc c", kc=2),
                    in_=d["wab_rev"][filt].rearrange("(kc p) c -> p kc c", p=128))
                sW.inc(inst, 16)
                wrev_mark[filt] = sW.v

            for uv in range(2):
                for fc in range(2):
                    for cg in range(4):
                        bk = sbank[0] % 2
                        sbank[0] += 1

                        @TEN
                        def _(tensor, filt=filt, uv=uv, fc=fc, cg=cg, bk=bk):
                            if uv == 0 and fc == 0 and cg == 0:
                                tensor.wait_ge(sW.h, wrev_mark[filt])
                                if filt == 0:
                                    tensor.wait_ge(sX.h, aconst)
                            km = roll["fwbank"][bk]
                            if km is not None:
                                tensor.wait_ge(sDV.h, km)
                            RB = RCB if uv == 0 else RSB
                            for kc in range(2):
                                inst = tensor.matmul(
                                    pFW[bk][:, 0:256],
                                    RB[:, kc * 256 + fc * 128:kc * 256 + (fc + 1) * 128],
                                    WREV[:, kc * C + cg * 256:kc * C + (cg + 1) * 256],
                                    start=(kc == 0), stop=(kc == 1))
                            sPE.inc(inst)
                            spec_mm[(filt, uv, fc, cg)] = sPE.v
                            spec_mm[(filt, "last")] = sPE.v

                        @VEC
                        def _(vector, filt=filt, uv=uv, fc=fc, cg=cg, bk=bk):
                            vector.wait_ge(sPE.h, spec_mm[(filt, uv, fc, cg)])
                            col = (uv * 2 + fc) * 2 * C + filt * C + cg * 256
                            inst = vector.tensor_copy(out=SPEC(col, 256),
                                                      in_=pFW[bk][:, 0:256])
                            sDV.inc(inst)
                            spec_ev[(filt, uv, fc, cg)] = sDV.v
                            roll["fwbank"][bk] = sDV.v
        spec_all = sDV.v

        emit_ln(first=False, scratch=XNBF2)

        def emit_ztile(i):
            @VEC
            def _(vector, i=i):
                if i >= 1:
                    vector.wait_ge(sPE.h, xtr_mm[i - 1])
                inst = vector.tensor_scalar(
                    out=XNBF2[:],
                    in0=X[:, i * C:(i + 1) * C],
                    scalar1=NEGM[:, i:i + 1], scalar2=RSTD[:, i:i + 1],
                    op0=ALU.add, op1=ALU.mult)
                sDV.inc(inst)
                xnb2_done[i] = sDV.v

            @TEN
            def _(tensor, i=i):
                tensor.wait_ge(sDV.h, xnb2_done[i])
                if i >= 1:
                    tensor.wait_ge(sDV.h, xtr_ev[i - 1])
                for j in range(NCC):
                    inst = tensor.transpose(
                        pXT[:, j * 128:(j + 1) * 128],
                        XNBF2[:, j * 128:(j + 1) * 128],
                        IDB[:])
                sPE.inc(inst)
                xtr_mm[i] = sPE.v

            @VEC
            def _(vector, i=i):
                vector.wait_ge(sPE.h, xtr_mm[i])
                if i >= 1:
                    vector.wait_ge(sPE.h, z_mm[(i - 1, 3)])
                inst = vector.tensor_copy(out=XN2T[:], in_=pXT[:])
                sDV.inc(inst)
                xtr_ev[i] = sDV.v

            for cg in range(4):
                @TEN
                def _(tensor, i=i, cg=cg):
                    if i == 0 and cg == 0:
                        tensor.wait_ge(sW.h, aw_done)
                    tensor.wait_ge(sDV.h, xtr_ev[i])
                    if (i, cg) != (0, 0):
                        pi, pc = (i, cg - 1) if cg else (i - 1, 3)
                        tensor.wait_ge(sDV.h, z_ev[(pi, pc)])
                    for hc in range(NCC):
                        tensor.matmul(pZ[:], XN2T[:, hc * 128:(hc + 1) * 128],
                                      WINT[:, hc * 2 * C + cg * 512:hc * 2 * C + (cg + 1) * 512],
                                      start=(hc == 0), stop=False)
                    inst = tensor.matmul(pZ[:], ONESB[0:1, :], BINRV[cg],
                                         start=False, stop=True)
                    sPE.inc(inst)
                    z_mm[(i, cg)] = sPE.v

                @VEC
                def _(vector, i=i, cg=cg):
                    vector.wait_ge(sPE.h, z_mm[(i, cg)])
                    if i == 0 and cg == 0:
                        vector.wait_ge(sPO.h, zhist)
                    if i >= 4 and cg == 0:
                        vector.wait_ge(sPE.h, fwd_done_w[(i - 2) // 2])
                    slot = (i + 2) % 4
                    inst = vector.tensor_copy(
                        out=ZBUFP[slot][:, cg * 512:(cg + 1) * 512],
                        in_=pZ[:])
                    sDV.inc(inst)
                    z_ev[(i, cg)] = sDV.v

        def emit_window(w):
            fbank = [0]
            for half in range(2):
                for fc in range(2):
                    for cg in range(2):
                        ch0 = [0, 1024][cg] + half * 512
                        for uv in range(2):
                            bk = fbank[0] % 2
                            fbank[0] += 1

                            @TEN
                            def _(tensor, w=w, half=half, fc=fc, cg=cg, uv=uv, ch0=ch0, bk=bk):
                                if half == 0 and fc == 0 and cg == 0 and uv == 0:
                                    tensor.wait_ge(sDV.h, z_ev[(2 * w + 1, 3)])
                                    if w == 0:
                                        tensor.wait_ge(sDV.h, spec_all)
                                    ym = roll["yfree"]
                                    if ym is not None:
                                        tensor.wait_ge(sPE.h, ym)
                                km = roll["fwbank"][bk]
                                if km is not None:
                                    tensor.wait_ge(sDV.h, km)
                                RB = RCB if uv == 0 else RSB
                                for tj in range(4):
                                    j = 2 * w - 2 + tj
                                    slot = (j + 2) % 4
                                    inst = tensor.matmul(
                                        pFW[bk][:],
                                        RB[:, tj * 256 + fc * 128:tj * 256 + (fc + 1) * 128],
                                        ZBUFP[slot][:, ch0:ch0 + 512],
                                        start=(tj == 0), stop=(tj == 3))
                                sPE.inc(inst)
                                fwd_mm[(w, half, fc, cg, uv)] = sPE.v

                            @VEC
                            def _(vector, w=w, half=half, fc=fc, cg=cg, uv=uv, bk=bk):
                                vector.wait_ge(sPE.h, fwd_mm[(w, half, fc, cg, uv)])
                                sl = roll["slice"].get((uv, cg))
                                if sl is not None:
                                    vector.wait_ge(sPO.h, sl)
                                inst = vector.tensor_copy(
                                    out=ZS[:, (uv * 2 + cg) * 512:(uv * 2 + cg) * 512 + 512],
                                    in_=pFW[bk][:])
                                sDV.inc(inst)
                                vector.wait_ge(sDV.h, sDV.v)
                                fwd_ev[(w, half, fc, cg, uv)] = sDV.v
                                roll["fwbank"][bk] = sDV.v

                        zu = ZS[:, (0 * 2 + cg) * 512:(0 * 2 + cg) * 512 + 512]
                        zv = ZS[:, (1 * 2 + cg) * 512:(1 * 2 + cg) * 512 + 512]
                        wu = SPEC((0 * 2 + fc) * 2 * C + ch0, 512)
                        wv = SPEC((1 * 2 + fc) * 2 * C + ch0, 512)
                        yuc = YU[:, fc * 1024 + cg * 512:fc * 1024 + (cg + 1) * 512]
                        yvc = YVH[fc * 2 + cg]

                        @VEC
                        def _(vector, w=w, half=half, fc=fc, cg=cg,
                              zu=zu, wu=wu, yuc=yuc):
                            inst = vector.tensor_tensor(out=yuc, in0=zu, in1=wu, op=ALU.mult)
                            sDV.inc(inst)
                            yu_mk[("p1", w, half, fc, cg)] = sDV.v

                        @GPS
                        def _(gpsimd, w=w, half=half, fc=fc, cg=cg,
                              zu=zu, zv=zv, wu=wu, wv=wv, yuc=yuc, yvc=yvc):
                            gpsimd.wait_ge(sDV.h, fwd_ev[(w, half, fc, cg, 1)])
                            inst = gpsimd.tensor_tensor(out=TMW[:], in0=zv, in1=wv, op=ALU.mult)
                            sPO.inc(inst)
                            inst = gpsimd.tensor_tensor(out=yvc, in0=zu, in1=wv, op=ALU.mult)
                            sPO.inc(inst)
                            inst = gpsimd.tensor_tensor(out=TMG[:], in0=zv, in1=wu, op=ALU.mult)
                            sPO.inc(inst)
                            gpsimd.wait_ge(sPO.h, sPO.v)
                            inst = gpsimd.tensor_tensor(out=yvc, in0=yvc, in1=TMG[:], op=ALU.add)
                            sPO.inc(inst)
                            gpsimd.wait_ge(sDV.h, yu_mk[("p1", w, half, fc, cg)])
                            inst = gpsimd.tensor_tensor(out=yuc, in0=yuc,
                                                        in1=TMW[:], op=ALU.subtract)
                            sPO.inc(inst)
                            if fc == 0:
                                gpsimd.wait_ge(sPO.h, sPO.v)
                                inst = gpsimd.tensor_tensor(out=yuc[0:1, :], in0=zu[0:1, :],
                                                            in1=wu[0:1, :], op=ALU.mult)
                                sPO.inc(inst)
                                inst = gpsimd.tensor_tensor(out=yvc[0:1, :], in0=zv[0:1, :],
                                                            in1=wv[0:1, :], op=ALU.mult)
                                sPO.inc(inst)
                            yu_mk[(w, half, fc, cg)] = sPO.v
                            yv_mk[(w, half, fc, cg)] = sPO.v
                            roll["slice"][(0, cg)] = sPO.v
                            roll["slice"][(1, cg)] = sPO.v

                sg_plan = {cil: sAC.v + cil + 1 for cil in range(4)}
                prev_gat = gat_mk.get(("last",))
                for cil in range(4):
                    ci = half * 4 + cil
                    for ab in range(2):
                        loc = cil if ab == 0 else 4 + cil

                        @TEN
                        def _(tensor, w=w, half=half, ci=ci, ab=ab, loc=loc, cil=cil):
                            if cil == 0 and ab == 0:
                                for fc in range(2):
                                    for cg in range(2):
                                        tensor.wait_ge(sPO.h, yu_mk[(w, half, fc, cg)])
                            km = ca_ev.get(("bank", ab))
                            if km:
                                tensor.wait_ge(sDV.h, km)
                            pb = pIV[ab]
                            for fc in range(2):
                                tensor.matmul(pb[:],
                                              YU[:, fc * 1024 + loc * 128:fc * 1024 + (loc + 1) * 128],
                                              RICB[:, fc * 256:(fc + 1) * 256],
                                              start=(fc == 0), stop=False)
                            for fc in range(2):
                                inst = tensor.matmul(pb[:],
                                                     YVH[fc * 2 + loc // 4][:, (loc % 4) * 128:(loc % 4 + 1) * 128],
                                                     RISB[:, fc * 256:(fc + 1) * 256],
                                                     start=False, stop=(fc == 1))
                            sPE.inc(inst)
                            iv_mm[(w, ci, ab)] = sPE.v
                            if cil == 3 and ab == 1:
                                roll["yfree"] = sPE.v

                        @VEC
                        def _(vector, w=w, half=half, ci=ci, cil=cil, ab=ab,
                              sg_plan=sg_plan):
                            vector.wait_ge(sPE.h, iv_mm[(w, ci, ab)])
                            if cil == 0 and ab == 0:
                                km = sg_mk.get(("last",))
                                if km:
                                    vector.wait_ge(sAC.h, km)
                            if cil >= 2 and ab == 0:
                                vector.wait_ge(sAC.h, sg_plan[cil - 2])
                            sl = cil % 2
                            bias = (BACOL if ab == 0 else BBCOL)[:, ci:ci + 1]
                            inst = vector.tensor_scalar(
                                out=CAB[:, sl * 512 + ab * 256:sl * 512 + (ab + 1) * 256],
                                in0=pIV[ab][:],
                                scalar1=bias, scalar2=None, op0=ALU.add)
                            sDV.inc(inst)
                            ca_ev[(w, ci, ab)] = sDV.v
                            ca_ev[("bank", ab)] = sDV.v

                    @VEC
                    def _(vector, w=w, half=half, cil=cil, sg_plan=sg_plan):
                        ci = half * 4 + cil
                        vector.wait_ge(sAC.h, sg_plan[cil])
                        if half == 0 and cil == 0:
                            km = op_mm.get(("gated", w - 1))
                            if km:
                                vector.wait_ge(sPE.h, km)
                        sl = cil % 2
                        vector.tensor_tensor(
                            out=GATED(ci * 256, 256),
                            in0=CAB[:, sl * 512:sl * 512 + 256],
                            in1=SGV[cil][:, 256:512], op=ALU.mult)
                        inst = vector.tensor_tensor(
                            out=GATED((8 + ci) * 256, 256),
                            in0=CAB[:, sl * 512 + 256:sl * 512 + 512],
                            in1=SGV[cil][:, 0:256], op=ALU.mult)
                        sDV.inc(inst)
                        gat_mk[(w, half)] = sDV.v
                        gat_mk[("last",)] = sDV.v

                @SCA
                def _(scalar, w=w, half=half, sg_plan=sg_plan, prev_gat=prev_gat):
                    if prev_gat:
                        scalar.wait_ge(sDV.h, prev_gat)
                    for cil in range(4):
                        scalar.wait_ge(sDV.h, ca_ev[(w, half * 4 + cil, 1)])
                        inst = scalar.activation(
                            out=SGV[cil],
                            in_=CAB[:, (cil % 2) * 512:(cil % 2 + 1) * 512],
                            func=AF.Sigmoid)
                        sAC.inc(inst)
                        assert sAC.v == sg_plan[cil]
                    sg_mk[(w, half)] = sAC.v
                    sg_mk[("last",)] = sAC.v


            for hc in range(NCC):
                @TEN
                def _(tensor, w=w, hc=hc):
                    if hc == 0:
                        if w == 0:
                            tensor.wait_ge(sW.h, aw_done)
                        for hf in range(2):
                            tensor.wait_ge(sDV.h, gat_mk[(w, hf)])
                    km = op_ev.get(("bank",))
                    if km:
                        tensor.wait_ge(sDV.h, km)
                    for ec in range(16):
                        inst = tensor.matmul(pOP[:],
                                             WOUTT[:, ec * C + hc * 128:ec * C + (hc + 1) * 128],
                                             GATED(ec * 256, 256),
                                             start=(ec == 0), stop=(ec == 15))
                    sPE.inc(inst)
                    op_mm[(w, hc)] = sPE.v
                    if hc == NCC - 1:
                        op_mm[("gated", w)] = sPE.v

                @VEC
                def _(vector, w=w, hc=hc):
                    vector.wait_ge(sPE.h, op_mm[(w, hc)])
                    km = at_mm.get(("atsb",))
                    if km:
                        vector.wait_ge(sPE.h, km)
                    inst = vector.tensor_scalar(out=ATSB[:], in0=pOP[:],
                                                scalar1=BOCOL[:, hc:hc + 1],
                                                scalar2=None, op0=ALU.add)
                    sDV.inc(inst)
                    op_ev[(w, hc)] = sDV.v
                    op_ev[("bank",)] = sDV.v

                @TEN
                def _(tensor, w=w, hc=hc):
                    tensor.wait_ge(sDV.h, op_ev[(w, hc)])
                    km = xadd.get(("bank",))
                    if km:
                        tensor.wait_ge(sDV.h, km)
                    for th in range(2):
                        inst = tensor.transpose(pAT[:, th * 128:(th + 1) * 128],
                                                ATSB[:, th * 128:(th + 1) * 128], IDF[:])
                    sPE.inc(inst)
                    at_mm[(w, hc)] = sPE.v
                    at_mm[("atsb",)] = sPE.v

                @VEC
                def _(vector, w=w, hc=hc):
                    vector.wait_ge(sPE.h, at_mm[(w, hc)])
                    for th in range(2):
                        i = 2 * w + th
                        xs = X[:, i * C + hc * 128:i * C + (hc + 1) * 128]
                        inst = vector.tensor_tensor(out=xs,
                                                    in0=pAT[:, th * 128:(th + 1) * 128],
                                                    in1=xs, op=ALU.add)
                        sDV.inc(inst)
                    xadd[(w, hc)] = sDV.v
                    xadd[("bank",)] = sDV.v

        for i in range(NT):
            emit_ztile(i)
            if i % 2 == 1:
                w = (i - 1) // 2
                emit_window(w)
                fwd_done_w[w] = max(v for k, v in fwd_mm.items() if k[0] == w)
        actx.close()
        return {i: xadd[(i // 2, NCC - 1)] for i in range(NT)}

    tile_done = emit_ffn(0)
    if DBG_PHASES >= 2:
        barrier()
        tile_done = emit_attn()
    if DBG_PHASES >= 3:
        barrier()
        tile_done = emit_ffn(1)

    @SYN
    def _(sync):
        for i in range(NT):
            sync.wait_ge(sDV.h, tile_done[i])
            inst = sync.dma_start(out=dout[i * 128:(i + 1) * 128, :],
                                  in_=X[:, i * C:(i + 1) * C])
            sST.inc(inst, 16)
        sync.wait_ge(sST.h, sST.v)

    blk.__exit__(None, None, None)
    ctx.close()
    return nc


def _dft_consts():
    t = np.arange(NFFT)
    f = np.arange(256)
    ang = 2 * np.pi * np.outer(t, f) / NFFT
    rc = np.cos(ang)
    rs = np.sin(ang)
    rs[:, 0] = np.cos(np.pi * t)
    tau = np.arange(256) + 256
    angi = 2 * np.pi * np.outer(f, tau) / NFFT
    ric = (2.0 / NFFT) * np.cos(angi)
    ric[0, :] = 1.0 / NFFT
    ris = (2.0 / NFFT) * np.sin(angi)
    ris[0, :] = ((-1.0) ** np.arange(256)) / NFFT
    return (rc.astype(BF), rs.astype(BF), ric.astype(BF), ris.astype(BF))


def _col128(v):
    n = v.shape[0] // 128
    return np.ascontiguousarray(v.reshape(n, 128).T)


def kernel(L0_attn_x, L0_attn_w_in, L0_attn_b_in, L0_attn_w_a, L0_attn_b_a,
           L0_attn_w_b, L0_attn_b_b, L0_attn_w_out, L0_attn_b_out,
           L0_ffn_router_w, L0_ffn_router_b, L0_ffn_w_c_fc, L0_ffn_b_c_fc,
           L0_ffn_w_gate, L0_ffn_b_gate, L0_ffn_w_c_proj, L0_ffn_b_c_proj):
    f32 = np.float32
    x = np.ascontiguousarray(np.asarray(L0_attn_x, f32))
    rc, rs, ric, ris = _dft_consts()
    base = {
        "rw_col": np.ascontiguousarray(
            np.asarray(L0_ffn_router_w, f32).reshape(NCC, 128, E)
            .transpose(1, 0, 2).reshape(128, NCC * E)),
        "rb_row": np.asarray(L0_ffn_router_b, f32).reshape(1, E),
        "iota_e": np.ascontiguousarray(
            np.broadcast_to(np.arange(E, dtype=f32), (128, E))),
        "tri_bf": np.triu(np.ones((128, 128), f32), 1),
        "ones_bf": np.ones((128, 128), f32),
        "wfc": np.ascontiguousarray(
            np.asarray(L0_ffn_w_c_fc, f32).reshape(E, NCC, 128, NHG, HGRP)
            .transpose(0, 3, 2, 1, 4).reshape(E, NHG, 128, NCC * HGRP)),
        "wg": np.ascontiguousarray(
            np.asarray(L0_ffn_w_gate, f32).reshape(E, NCC, 128, NHG, HGRP)
            .transpose(0, 3, 2, 1, 4).reshape(E, NHG, 128, NCC * HGRP)),
        "wp": np.ascontiguousarray(
            np.asarray(L0_ffn_w_c_proj, f32).reshape(E, NHC, 128, NCC, 128)
            .transpose(0, 3, 2, 1, 4).reshape(E, NCC, 128, NHC * 128)),
        "bfc_col": np.ascontiguousarray(
            np.asarray(L0_ffn_b_c_fc, f32).reshape(E, NHC, 128)
            .transpose(2, 0, 1).reshape(128, E * NHC)),
        "bg_col": np.ascontiguousarray(
            np.asarray(L0_ffn_b_gate, f32).reshape(E, NHC, 128)
            .transpose(2, 0, 1).reshape(128, E * NHC)),
        "bp_col": np.ascontiguousarray(
            np.asarray(L0_ffn_b_c_proj, f32).reshape(E, NCC, 128)
            .transpose(2, 0, 1).reshape(128, E * NCC)),
        "w_inT": np.ascontiguousarray(np.asarray(L0_attn_w_in, f32).T),
        "b_in_row": np.asarray(L0_attn_b_in, f32).reshape(1, 2 * C),
        "w_outT": np.ascontiguousarray(np.asarray(L0_attn_w_out, f32).T),
        "b_out_col": _col128(np.asarray(L0_attn_b_out, f32)),
        "wab_rev": np.ascontiguousarray(np.stack([
            np.asarray(L0_attn_w_a, f32)[:, 0, ::-1].T,
            np.asarray(L0_attn_w_b, f32)[:, 0, ::-1].T])),
        "b_a_col": _col128(np.asarray(L0_attn_b_a, f32)),
        "b_b_col": _col128(np.asarray(L0_attn_b_b, f32)),
        "rc": rc, "rs": rs, "ric": ric, "ris": ris,
    }
    nc = build_nc()
    in_maps = [dict(base, x=np.ascontiguousarray(x[b])) for b in range(8)]
    res = run_bass_kernel_spmd(nc, in_maps, core_ids=list(range(8)))
    return np.stack([np.asarray(r["out"]) for r in res.results], axis=0)
